# version 13 (final)
# baseline (speedup 1.0000x reference)
import sys
import numpy as np
import ml_dtypes
import concourse.bass as bass
import concourse.mybir as mybir
from concourse.bass_utils import run_bass_kernel_spmd

F32 = mybir.dt.float32
F32R = mybir.dt.float32r
BF16 = mybir.dt.bfloat16
U8 = mybir.dt.uint8
AF = mybir.ActivationFunctionType
ALU = mybir.AluOpType
AX = mybir.AxisListType

T = 2048
D = 1024
DT = 8
H = 16
F1 = 4096
FT = 32
NEG = -1.0e30
KB = 1024
Z = ("SY", 0)


class P:
    def __init__(self, nc):
        self.nc = nc
        self.ops = {e: [] for e in ("tensor", "vector", "scalar", "gpsimd", "sync")}
        self.sems = {}
        self.cnt = {}
        self.bankw = [tuple() for _ in range(8)]
        self.hwm = {e: {} for e in self.ops}
        self.contrib = {}

    def sem(self, name):
        self.sems[name] = self.nc.alloc_semaphore(name=name)
        self.cnt[name] = 0
        self.contrib[name] = {}

    def _filter(self, eng, waits):
        best = {}
        for s, v in waits:
            if v > 0 and v > best.get(s, 0):
                best[s] = v
        hw = self.hwm[eng]
        semrefs = []
        for s, v in best.items():
            if v <= hw.get(s, 0):
                continue
            hw[s] = v
            if v <= self.contrib[s].get(eng, 0):
                continue
            semrefs.append((self.sems[s], v))
        return tuple(semrefs)

    def emit(self, eng, build, waits=(), inc=None, incn=1):
        semrefs = self._filter(eng, waits)
        inch = self.sems[inc] if inc else None

        def f(e, _b=build, _w=semrefs, _i=inch, _n=incn):
            for h, v in _w:
                e.wait_ge(h, v)
            ins = _b(e)
            if _i is not None:
                ins.then_inc(_i, _n)
        self.ops[eng].append(f)
        if inc:
            self.cnt[inc] += incn
            if incn == 1:
                self.contrib[inc][eng] = self.contrib[inc].get(eng, 0) + 1
            return (inc, self.cnt[inc])
        return None

    def wait(self, eng, waits):
        semrefs = self._filter(eng, waits)

        def f(e, _w=semrefs):
            for h, v in _w:
                e.wait_ge(h, v)
        self.ops[eng].append(f)

    def v(self, name):
        return (name, self.cnt[name])

    def fillw(self, bk, extra=()):
        w = list(self.bankw[bk]) + list(extra)
        self.bankw[bk] = tuple()
        return w

    def consumed(self, bk, mark):
        self.bankw[bk] = (mark,)


def host_constants():
    c = {}
    j = np.arange(128)[:, None]
    q = np.arange(512)[None, :]
    bm = np.zeros((8, 128, 512), dtype=np.uint8)
    for r in range(8):
        delta = -256 + 128 * r
        bm[r] = (np.abs(delta + j - q) <= 256).astype(np.uint8)
    c["band_masks"] = np.ascontiguousarray(bm.transpose(1, 0, 2).reshape(128, 8 * 512))
    cm = np.zeros((4, 128, 512), dtype=np.uint8)
    for r in range(4):
        cm[r] = ((128 * r + j) <= q).astype(np.uint8)
    c["cst_masks"] = np.ascontiguousarray(cm.transpose(1, 0, 2).reshape(128, 4 * 512))
    qq = np.arange(128)[:, None]
    jj = np.arange(128)[None, :]
    c["qmaj_mask"] = np.where(jj <= qq, 0.0, np.float32(NEG)).astype(np.float32)
    bf = np.zeros((3, 128, 640), dtype=np.float32)
    r_ = np.arange(128)[:, None]
    c_ = np.arange(640)[None, :]
    for oi, off in enumerate((0, 128, 256)):
        bf[oi] = np.where(np.abs(r_ + off - c_) <= 256, 0.0, np.float32(NEG))
    c["bandf_masks"] = np.ascontiguousarray(bf.transpose(1, 0, 2).reshape(128, 3 * 640)).astype(np.float32)
    c["i128"] = np.eye(128, dtype=np.float32)
    return c


def build_core(nc, n_sub=4, seqs=4, dump_x=False, debug_stage=None, chain=("attn_a", "ffn", "attn_b", "ffn"), paranoid=0):
    xin = nc.dram_tensor("x", [seqs, 128, DT * T], F32, kind="ExternalInput").ap()
    y = nc.dram_tensor("y", [seqs, 128, DT * T], F32, kind="ExternalOutput").ap()
    wqkv_a = nc.dram_tensor("wqkv_a", [D, 3 * D], F32, kind="ExternalInput").ap()
    bqkv_a = nc.dram_tensor("bqkv_a", [3 * D, 1], F32, kind="ExternalInput").ap()
    smallp_in = nc.dram_tensor("smallpack", [128, 32], F32, kind="ExternalInput").ap()
    wg8_in = nc.dram_tensor("wg8", [128, 8], F32, kind="ExternalInput").ap()
    wout_a = nc.dram_tensor("wout_a", [D, D], F32, kind="ExternalInput").ap()
    bout_a = nc.dram_tensor("bout_a", [D, 1], F32, kind="ExternalInput").ap()
    wg_a = nc.dram_tensor("wg_a", [D, 1], F32, kind="ExternalInput").ap()
    bg_a = nc.dram_tensor("bg_a", [1, 1], F32, kind="ExternalInput").ap()
    wqkv_b = nc.dram_tensor("wqkv_b", [D, 3 * D], F32, kind="ExternalInput").ap()
    wproj_b = nc.dram_tensor("wproj_b", [D, D], F32, kind="ExternalInput").ap()
    k0a = nc.dram_tensor("k0a", [D, F1], U8, kind="ExternalInput").ap()
    k0b = nc.dram_tensor("k0b", [D, F1], U8, kind="ExternalInput").ap()
    k1 = nc.dram_tensor("k1", [F1, F1], U8, kind="ExternalInput").ap()
    k2 = nc.dram_tensor("k2", [F1, D], U8, kind="ExternalInput").ap()
    s0in = nc.dram_tensor("s0", [1, 1], F32, kind="ExternalInput").ap()
    s1in = nc.dram_tensor("s1", [1, 1], F32, kind="ExternalInput").ap()
    s2in = nc.dram_tensor("s2", [1, 1], F32, kind="ExternalInput").ap()
    band_m = nc.dram_tensor("band_masks", [128, 8 * 512], U8, kind="ExternalInput").ap()
    bandf_m = nc.dram_tensor("bandf_masks", [128, 3 * 640], F32, kind="ExternalInput").ap()
    cst_m = nc.dram_tensor("cst_masks", [128, 4 * 512], U8, kind="ExternalInput").ap()
    qmaj_m = nc.dram_tensor("qmaj_mask", [128, 128], F32, kind="ExternalInput").ap()
    i128_in = nc.dram_tensor("i128", [128, 128], F32, kind="ExternalInput").ap()
    xbuf = nc.dram_tensor("xbuf", [seqs, 128, DT * T], F32, kind="Internal").ap()
    v8scr = nc.dram_tensor("v8scr", [1, 1024], F32, kind="Internal").ap()
    w0s = nc.dram_tensor("w0s", [DT, 128, F1], BF16, kind="Internal").ap()
    w1s = nc.dram_tensor("w1s", [8, FT, 128, 512], BF16, kind="Internal").ap()
    w2s = nc.dram_tensor("w2s", [FT, 128, D], BF16, kind="Internal").ap()
    xdump = nc.dram_tensor("xdump", [n_sub, seqs, 128, DT * T], F32,
                           kind="ExternalOutput").ap() if dump_x else None
    enddump = nc.dram_tensor("enddump", [8, 128, 1024], F32, kind="ExternalOutput").ap() if dump_x else None
    dbg = nc.dram_tensor("dbg", [4, 128, DT * T], F32, kind="ExternalOutput").ap() \
        if debug_stage else None

    p = P(nc)
    for sname in ("PE", "DV", "AC", "GP",
                  "CS", "CG", "CV",
                  "CA0", "CA1",
                  "WQ", "WO",
                  "W00", "W01", "W10", "W11", "W2",
                  "XG0", "XG1", "XC0", "XC1",
                  "XS0", "XS1", "XO",
                  "XF0", "XF1", "XF2", "XF3", "XF4", "XF5", "XF6", "XF7",
                  "MR", "NM", "V8", "GR", "XD"):
        p.sem(sname)

    arena = nc.alloc_sbuf_tensor("arena", [128, 212800], U8)
    base = nc.lookup_mloc(arena).addr
    _off = [0]

    def alloc(name, shape, dtype, align=32):
        _off[0] = (_off[0] + align - 1) // align * align
        h = nc.alloc_sbuf_tensor_at(name, list(shape), dtype, offset=base + _off[0])
        _off[0] += int(np.prod(shape[1:])) * mybir.dt.size(dtype)
        return h

    band = alloc("band", [128, 8 * 512], U8, align=512)
    cst = alloc("cst", [128, 4 * 512], U8, align=512)
    qmaj = alloc("qmaj", [128, 256], BF16, align=512)
    i128 = alloc("i128b", [128, 256], BF16, align=512)
    bandf = alloc("bandf", [128, 2048], BF16, align=512)
    smallp = alloc("smallp", [128, 128], F32, align=512)
    wg_bf = alloc("wg_bf", [128, 256], BF16, align=512)
    bq_s = smallp[:, 0:8]
    bk_s = smallp[:, 8:16]
    bo_pp = smallp[:, 16:24]
    srow = smallp[0:1, 24:32]
    onesc = alloc("onesc", [128, 1], F32, align=512)
    onescb = alloc("onescb", [128, 1], BF16)
    onesr = alloc("onesr", [128, 128], F32)
    bv_b = alloc("bv_b", [128, 1024], BF16)
    sc01 = alloc("sc01", [128, 1], F32)
    sc12 = alloc("sc12", [128, 1], F32)
    nbg = alloc("nbg", [1, 1], F32)
    c_eps = alloc("c_eps", [1, 1], F32)
    lnal = alloc("lnal", [1, 1024], F32)
    lnbe = alloc("lnbe", [1, 1024], F32)
    lnt = alloc("lnt", [128, 2 * 512], F32)
    lnm = lnt[0:1, 0:1024]
    m4 = alloc("m4", [128, 16], BF16)
    onesbr = alloc("onesbr", [128, 128], BF16)
    per = (_off[0] + 31) // 32 * 32

    def at(name, shape, dtype, off):
        esz = mybir.dt.size(dtype)
        boff = per + off
        assert boff % esz == 0, (name, boff, esz)
        cols = int(np.prod(shape[1:]))
        return arena[:, :].bitcast(dtype)[:, boff // esz: boff // esz + cols]

    qT = at("qT", [128, DT * T], BF16, 0)
    kT = at("kT", [128, DT * T], BF16, 32 * KB)
    vtok = at("vtok", [128, H * 16 * 65], BF16, 64 * KB)
    hT = at("hT", [128, DT * 1024], BF16, 97 * KB)
    wout_sb = at("wout_sb", [128, DT * 1024], BF16, 97 * KB)
    sq = at("sq", [128, 2 * 1024], BF16, 113 * KB)
    ebuf = at("ebuf", [128, 6 * 512], BF16, 117 * KB)
    epi = at("epi", [128, 2 * 512], F32, 117 * KB)
    wbuf = at("wbuf", [128, 8 * 512], BF16, 123 * KB)
    xc = at("xc", [128, 2 * 512], F32, 131 * KB)
    o_qg = at("o_qg", [128, DT * 512], BF16, 135 * KB)
    xg = at("xg", [128, 2 * 1024], F32, 135 * KB)
    grow = at("grow", [128, T], F32, 143 * KB)
    urow = at("urow", [128, T], F32, 151 * KB)
    wbuf2 = at("wbuf2", [128, 8 * 512], BF16, 151 * KB)
    v8 = at("v8", [128, 1024], F32, 159 * KB)
    mrow = at("mrow", [128, 1024], F32, 163 * KB + 128)
    wrow = at("wrow", [128, 1024], F32, 167 * KB + 128)
    erow = [at(f"erow{i}", [128, 512], F32, 171 * KB + 128 + 2048 * i) for i in range(3)]
    nmrow = at("nmrow", [128, 2048], BF16, 113 * KB)
    ngb = at("ngb", [128, 1024], F32, 113 * KB)
    mrow2 = at("mrow2", [128, 1024], F32, 143 * KB + 128)
    wrow2 = at("wrow2", [128, 1024], F32, 147 * KB + 128)
    v8c = at("v8c", [128, 16], F32, 171 * KB + 128 + 6144 + 384)
    nla = at("nla", [128, 1024], F32, 163 * KB + 128)
    nga = at("nga", [128, 1024], F32, 167 * KB + 128)
    stga = at("stga", [128, 128], F32, 143 * KB + 128)
    attn_end = 171 * KB + 128 + 2048 * 3 + 384 + 512
    a1T = at("a1T", [128, FT * 1024], BF16, 0)
    h0T = at("h0T", [128, DT * 1024], BF16, 64 * KB)
    outh = at("outh", [128, DT * 1024], F32, 80 * KB)
    a2s = at("a2s", [128, 2 * 2048], BF16, 112 * KB)
    sqf = at("sqf", [128, 2 * 1024], BF16, 112 * KB)
    w1b = at("w1b", [128, 2 * FT * 256], BF16, 120 * KB)
    w2b = at("w2b", [128, 4 * 1024], BF16, 152 * KB)
    w0b = at("w0b", [128, 2 * 8 * 512], BF16, 160 * KB)
    ffn_end = 176 * KB
    assert per + max(attn_end, ffn_end) <= 212800, (per, attn_end, ffn_end)

    pball = nc.alloc_psum_tensor("pball", [128, 4096], F32)
    pb = [pball[:, 512 * i:512 * (i + 1)] for i in range(8)]
    vre = vtok[:, :].rearrange("p (h c i) -> p h c i", h=H, c=16, i=65)

    nc._mm_count = [0]
    nc._phase_marks = []
    nc._mm_lines = []

    def phase(label):
        nc._phase_marks.append((nc._mm_count[0], label))

    def mm(out, lhsT, rhs, start, stop, waits=(), inc=None, tp=None):
        nc._mm_count[0] += 1
        nc._mm_lines.append((sys._getframe(1).f_lineno, out.shape[0], out.shape[1]))
        return p.emit("tensor",
                      lambda e, a=out, l=lhsT, r=rhs, s=start, _t=stop, t=tp:
                      e.matmul(a, l, r, start=s, stop=_t, tile_position=t),
                      waits=waits, inc=inc)

    st = {
        "wbufs": [Z, Z], "hT_pe": Z, "vtok": Z, "qkT": Z, "sq": [Z, Z], "xg": [Z, Z],
        "xc": [Z, Z], "ebuf": [Z, Z, Z, Z, Z, Z], "epi": Z, "o_qg_w": Z, "o_qg_r": Z,
        "grow_dv": Z, "grow_sy": Z, "grow_use": Z, "v8_sy": Z, "v8_use": Z,
        "mrow": [Z, Z], "wrow_stage": [Z, Z], "wrow64": [Z, Z], "ngb": Z, "nla": Z,
        "erow": Z, "erow_pe": Z, "nm_use": Z,
        "a2s": [Z, Z], "w1b": [Z, Z], "w0b": [Z, Z], "w2b": Z,
        "outh_sy": Z, "lnrows": Z, "lnt": [Z, Z], "m4": Z, "wout": Z, "conv": Z, "xw": [],
    }

    def _init(e):
        e.memset(smallp[:], 0.0)
        e.memset(qmaj[:], 0.0)
        e.memset(i128[:], 0.0)
        e.memset(bandf[:], 0.0)
        e.memset(wg_bf[:], 0.0)
        return e.memset(bv_b[:], 0.0)
    dvinit = p.emit("vector", _init, inc="DV")
    p.emit("sync", lambda e: e.dma_start(band[:], band_m), inc="CS", incn=16)
    p.emit("gpsimd", lambda e: e.dma_start(bandf[:, 0:1920], bandf_m), waits=[dvinit], inc="CG", incn=16)
    p.emit("sync", lambda e: e.dma_start(cst[:], cst_m), inc="CS", incn=16)
    p.emit("gpsimd", lambda e: e.dma_start(qmaj[:, 0:128], qmaj_m), inc="CG", incn=16)
    p.emit("gpsimd", lambda e: e.dma_start(i128[:, 0:128], i128_in), inc="CG", incn=16)
    p.emit("gpsimd", lambda e: e.dma_start(wg_bf[:, 0:8], wg8_in), inc="CG", incn=16)
    gq_bv = p.emit("gpsimd", lambda e: e.dma_start(bv_b[0:1, :], bqkv_a[2048:3072, :].rearrange("a b -> b a")), inc="CG", incn=16)
    sy0 = p.emit("sync", lambda e: e.dma_start(smallp[:, 0:32], smallp_in), waits=[dvinit], inc="CS", incn=16)
    p.emit("vector", lambda e: e.memset(onesc[:], 1.0))
    p.emit("vector", lambda e: e.memset(onescb[:], 1.0))
    p.emit("vector", lambda e: e.memset(onesr[:], 1.0))
    p.emit("vector", lambda e: e.memset(c_eps[:], 1.0e-5))
    p.emit("vector", lambda e: e.memset(onesbr[:], 1.0))

    def _sc(e):
        e.tensor_scalar_mul(srow[0:1, 4:5], srow[0:1, 0:1], 4.0)
        e.tensor_scalar(srow[0:1, 5:6], srow[0:1, 1:2], srow[0:1, 2:3], 2.0, ALU.mult, ALU.mult)
        e.tensor_scalar_mul(nbg[:], srow[0:1, 3:4], -1.0)
        return e.tensor_scalar_mul(bq_s[:], bq_s[:], 0.125)
    dv0 = p.emit("vector", _sc, waits=[sy0], inc="DV")
    mm(pb[0][:, 0:1], onesr[0:1, :], srow[0:1, 4:5], True, True, waits=[dv0])
    mm(pb[3][:, 0:1], onesr[0:1, :], srow[0:1, 5:6], True, True)
    dvob = p.emit("vector", lambda e: e.engine_nop(), waits=[gq_bv], inc="DV")
    mm(pb[1], onesbr[0:1, :], bv_b[0:1, 0:512], True, True, waits=[dvob])
    pe0 = mm(pb[2], onesbr[0:1, :], bv_b[0:1, 512:1024], True, True, inc="PE")

    def _cpb(e):
        e.tensor_copy(sc01[:], pb[0][:, 0:1])
        e.tensor_copy(sc12[:], pb[3][:, 0:1])
        e.tensor_copy(bv_b[:, 0:512], pb[1])
        return e.tensor_copy(bv_b[:, 512:1024], pb[2])
    dvc = p.emit("vector", _cpb, waits=[pe0], inc="DV")
    for bk in (0, 1, 2, 3):
        p.consumed(bk, dvc)

    ca = [at("ca0", [128, 4096], U8, 0), at("ca1", [128, 4096], U8, 4096)]
    cb = [at("cb0", [128, 4096], U8, 8192), at("cb1", [128, 4096], U8, 12288)]
    co = [at("co0", [128, 4096], BF16, 16384), at("co1", [128, 4096], BF16, 24576)]
    jobs = [("w0", k) for k in range(DT)] + [("w1", k) for k in range(FT)] + [("w2", k) for k in range(FT)]
    last_store = {0: Z, 1: Z}
    for ji, (kind, kk) in enumerate(jobs):
        par = ji % 2
        wld = [last_store[par]]
        if kind == "w0":
            p.emit("gpsimd", lambda e, kk=kk, par=par: e.dma_start(ca[par][:], k0a[128 * kk:128 * (kk + 1), :]),
                   waits=wld, inc="CA%d" % par, incn=16)
            lg = p.emit("gpsimd", lambda e, kk=kk, par=par: e.dma_start(cb[par][:], k0b[128 * kk:128 * (kk + 1), :]),
                        inc="CA%d" % par, incn=16)
            dv = p.emit("vector", lambda e, par=par: e.tensor_tensor(co[par][:], ca[par][:], cb[par][:], ALU.subtract),
                        waits=[lg], inc="DV")
            last_store[par] = p.emit("sync", lambda e, kk=kk, par=par: e.dma_start(w0s[kk], co[par][:]),
                                     waits=[dv], inc="CV", incn=16)
        else:
            src, ncol = (k1, F1) if kind == "w1" else (k2, D)
            lg = p.emit("gpsimd", lambda e, kk=kk, par=par, src=src, ncol=ncol:
                        e.dma_start(ca[par][:, 0:ncol], src[128 * kk:128 * (kk + 1), :]),
                        waits=wld, inc="CA%d" % par, incn=16)
            dv = p.emit("vector", lambda e, par=par, ncol=ncol:
                        e.tensor_scalar(co[par][:, 0:ncol], ca[par][:, 0:ncol], 2.0, -1.0, ALU.mult, ALU.add),
                        waits=[lg], inc="DV")
            if kind == "w1":
                for s2 in range(8):
                    last_store[par] = p.emit("sync", lambda e, kk=kk, par=par, s2=s2:
                                             e.dma_start(w1s[s2, kk], co[par][:, 512 * s2:512 * (s2 + 1)]),
                                             waits=[dv] if s2 == 0 else [], inc="CV", incn=16)
            else:
                last_store[par] = p.emit("sync", lambda e, kk=kk, par=par:
                                         e.dma_start(w2s[kk], co[par][:, 0:D]),
                                         waits=[dv], inc="CV", incn=16)
    st["conv"] = p.v("CV")
    _marks = [p.v("CV"), p.v("CS"), p.v("CG")]
    p.wait("tensor", _marks)
    p.wait("vector", _marks)
    p.wait("scalar", _marks)

    DMAS = ("CS", "CG", "CV", "CA0", "CA1", "WQ", "WO", "W00", "W01", "W10", "W11", "W2",
            "XG0", "XG1", "XC0", "XC1", "XS0", "XS1", "XO",
            "XF0", "XF1", "XF2", "XF3", "XF4", "XF5", "XF6", "XF7", "MR", "NM", "V8", "GR")

    def joinall():
        marks = [p.v(x) for x in ("PE", "DV", "AC") + DMAS]
        for eng in ("tensor", "vector", "scalar", "sync", "gpsimd"):
            p.wait(eng, marks)

    def emit_ln(seq, half, xsrc, ffn_mode):
        phase(f"LN{'f' if ffn_mode else 'a'}")
        hb = half * 1024
        sqb = sqf if ffn_mode else sq
        dst = h0T if ffn_mode else hT
        xg_pe = list(st["xg"])
        for k in range(DT):
            par = k % 2
            if ffn_mode:
                xv = outh[:, 1024 * k:1024 * (k + 1)]
                lw = []
            else:
                xv = xg[:, par * 1024:(par + 1) * 1024]
                lw = [xg_pe[par], st["o_qg_r"]]
            lx = p.emit("sync", lambda e, xv=xv, k=k: e.dma_start(xv, xsrc[seq, :, T * k + hb:T * k + hb + 1024]),
                        waits=lw + st["xw"], inc=("XF%d" % k) if ffn_mode else ("XG%d" % par), incn=16)
            sv = sqb[:, par * 1024:(par + 1) * 1024]
            sqw = [st["sq"][par]]
            if k < 2:
                sqw += [st["a2s"][0], st["a2s"][1]]
            ac = p.emit("scalar", lambda e, sv=sv, xv=xv: e.activation(sv, xv, AF.Square),
                        waits=[lx] + sqw, inc="AC")
            last = None
            for c in range(2):
                mm(pb[4 + c][0:1, :], onesc[:, 0:1],
                   xv[:, 512 * c:512 * (c + 1)], k == 0, k == DT - 1,
                   waits=(p.fillw(4 + c) + [lx]) if k == 0 else [])
                last = mm(pb[6 + c][0:1, :], onescb[:, 0:1], sv[:, 512 * c:512 * (c + 1)],
                          k == 0, k == DT - 1,
                          waits=(p.fillw(6 + c) + [ac]) if k == 0 else ([ac] if c == 0 else []),
                          inc="PE" if c == 1 else None)
            xg_pe[par] = last
            st["sq"][par] = last
        if not ffn_mode:
            st["xg"] = xg_pe
        pe_stats = p.v("PE")

        def _rows(e):
            e.tensor_scalar_mul(lnm[0:1, 0:512], pb[4][0:1, :], 1.0 / D)
            e.tensor_scalar_mul(lnm[0:1, 512:1024], pb[5][0:1, :], 1.0 / D)
            e.tensor_scalar_mul(lnal[0:1, 0:512], pb[6][0:1, :], 1.0 / D)
            e.tensor_scalar_mul(lnal[0:1, 512:1024], pb[7][0:1, :], 1.0 / D)
            e.tensor_tensor(lnbe[0:1, :], lnm[0:1, :], lnm[0:1, :], ALU.mult)
            return e.tensor_tensor(lnal[0:1, :], lnal[0:1, :], lnbe[0:1, :], ALU.subtract)
        dvr = p.emit("vector", _rows, waits=[pe_stats, st["lnrows"], st["lnt"][0], st["lnt"][1]], inc="DV")
        for bk in (4, 5, 6, 7):
            p.consumed(bk, dvr)
        acs = p.emit("scalar", lambda e: e.activation(lnbe[0:1, :], lnal[0:1, :], AF.Sqrt, bias=c_eps[0:1, 0:1]),
                     waits=[dvr], inc="AC")

        def _rows2(e):
            e.reciprocal(lnal[0:1, :], lnbe[0:1, :])
            return e.scalar_tensor_tensor(lnbe[0:1, :], lnm[0:1, :], -1.0, lnal[0:1, :], ALU.mult, ALU.mult)
        dvr2 = p.emit("vector", _rows2, waits=[acs], inc="DV")
        st["lnrows"] = dvr2
        dv_last = Z
        first_h = True
        for c in range(2):
            ba, bb_ = 4 + 2 * c, 5 + 2 * c
            mm(pb[ba], onesr[0:1, :], lnal[0:1, 512 * c:512 * (c + 1)],
               True, True, waits=p.fillw(ba) + [dvr2])
            peb = mm(pb[bb_], onesr[0:1, :], lnbe[0:1, 512 * c:512 * (c + 1)],
                     True, True, waits=p.fillw(bb_), inc="PE")
            for k in range(DT):
                par = k % 2
                if ffn_mode:
                    xv = outh[:, 1024 * k + 512 * c:1024 * k + 512 * (c + 1)]
                    lxw = []
                else:
                    xv = xc[:, 512 * par:512 * (par + 1)]
                    lx = p.emit("sync", lambda e, xv=xv, k=k, c=c:
                                e.dma_start(xv, xsrc[seq, :, T * k + hb + 512 * c:T * k + hb + 512 * (c + 1)]),
                                waits=[st["xc"][par]] + st["xw"], inc="XC%d" % par, incn=16)
                    lxw = [lx]
                lt = lnt[:, 512 * par:512 * par + 512]
                w1_ = lxw + ([peb] if k == 0 else []) + [st["lnt"][par]]
                p.emit("vector", lambda e, xv=xv, lt=lt, ba=ba: e.tensor_tensor(lt, xv, pb[ba], ALU.mult),
                       waits=w1_)
                hw = []
                if first_h:
                    hw = [st["outh_sy"], st["hT_pe"]]
                    first_h = False
                dv_last = p.emit("vector", lambda e, lt=lt, k=k, c=c, bb_=bb_:
                                 e.tensor_tensor(dst[:, 1024 * k + 512 * c:1024 * k + 512 * (c + 1)], lt, pb[bb_], ALU.add),
                                 waits=hw, inc="DV")
                st["lnt"][par] = dv_last
                if not ffn_mode:
                    st["xc"][par] = dv_last
            p.consumed(ba, dv_last)
            p.consumed(bb_, dv_last)
            st["lnrows"] = peb
            if c == 0:
                dv_c0 = dv_last
        return dv_c0, dv_last

    def emit_qkv(seq, half, is_a, hdv0, hdv1):
        phase("QKV")
        hb = half * 1024
        wsrc = wqkv_a if is_a else wqkv_b
        bufs = (wbuf, wbuf2)
        st["wbufs"] = list(st["wbufs"])
        loaded = [None] * 6

        def qload(scc):
            par = scc % 2
            buf = bufs[par]
            c0 = 512 * scc
            marks = []
            for kk in range(DT):
                marks.append(p.emit("gpsimd", lambda e, kk=kk, c0=c0, buf=buf:
                             e.dma_start(buf[:, 512 * kk:512 * (kk + 1)],
                                         wsrc[128 * kk:128 * (kk + 1), c0:c0 + 512]),
                             waits=[st["wbufs"][par]] if kk == 0 else [], inc="WQ", incn=16))
            loaded[scc] = marks
        qload(0)
        qload(1)
        for sc in range(4):
            par = sc % 2
            buf = bufs[par]
            marks = loaded[sc]
            for tc in range(2):
                for dp in range(2):
                    for dd in range(2):
                        dt_sc = 2 * dp + dd
                        bk = 4 * dp + 2 * dd + tc
                        pe_f = Z
                        for kk in range(DT):
                            pe_f = mm(pb[bk],
                                      buf[:, 512 * kk + 128 * dt_sc:512 * kk + 128 * (dt_sc + 1)],
                                      hT[:, 1024 * kk + 512 * tc:1024 * kk + 512 * (tc + 1)],
                                      kk == 0, kk == DT - 1,
                                      waits=(p.fillw(bk) + [marks[kk], hdv0 if tc == 0 else hdv1])
                                      if kk == 0 else [marks[kk]],
                                      inc="PE" if kk == DT - 1 else None)
                        dt_glob = 4 * sc + dt_sc
                        dst = qT if dt_glob < 8 else kT
                        dt_l = dt_glob % 8
                        col = T * dt_l + hb + 512 * tc
                        scale = 0.125 if dt_glob < 8 else 1.0
                        if is_a:
                            bias = (bq_s if dt_glob < 8 else bk_s)[:, dt_l:dt_l + 1]
                            ac = p.emit("vector", lambda e, bk=bk, dst=dst, col=col, bias=bias, scale=scale:
                                        e.tensor_scalar(dst[:, col:col + 512], pb[bk], scale, bias,
                                                        ALU.mult, ALU.add),
                                        waits=[pe_f], inc="DV")
                        else:
                            ac = p.emit("scalar", lambda e, bk=bk, dst=dst, col=col, scale=scale:
                                        e.activation(dst[:, col:col + 512], pb[bk], AF.Copy, scale=scale),
                                        waits=[pe_f], inc="AC")
                        p.consumed(bk, ac)
            st["wbufs"][par] = p.v("PE")
            if sc + 2 < 6:
                qload(sc + 2)
        for sc in range(2):
            par = sc % 2
            buf = bufs[par]
            marks = loaded[4 + sc]
            for tt in range(8):
                ttg = 8 * half + tt
                bk = tt
                pe_f = Z
                for kk in range(DT):
                    pe_f = mm(pb[bk],
                              hT[:, 1024 * kk + 128 * tt:1024 * kk + 128 * (tt + 1)],
                              buf[:, 512 * kk:512 * (kk + 1)],
                              kk == 0, kk == DT - 1,
                              waits=(p.fillw(bk) + [marks[kk], hdv0 if tt < 4 else hdv1])
                              if kk == 0 else [marks[kk]],
                              inc="PE" if kk == DT - 1 else None)
                dstv = vre[:, 8 * sc:8 * (sc + 1), ttg, 0:64]
                srcv = pb[bk].rearrange("p (h i) -> p h i", h=8, i=64)

                def _ev(e, dstv=dstv, srcv=srcv, sc=sc, is_a=is_a):
                    if is_a:
                        bvv = bv_b[:, 512 * sc:512 * (sc + 1)].rearrange("p (h i) -> p h i", h=8, i=64)
                        return e.tensor_tensor(dstv, srcv, bvv, ALU.add)
                    return e.tensor_copy(dstv, srcv)
                dv = p.emit("vector", _ev, waits=[pe_f, st["outh_sy"]], inc="DV")
                p.consumed(bk, dv)
            st["wbufs"][par] = p.v("PE")
        if half == 1:
            st["vtok"] = p.emit("vector", lambda e: e.memset(vre[:, :, :, 64:65], 1.0), inc="DV")
        return

    def emit_gate(seq, hT_dv, half):
        hb = half * 1024
        for c in range(2):
            bk = 4 + c
            pe_f = Z
            for kk in range(DT):
                pe_f = mm(pb[bk][0:1, :], wg_bf[:, kk:kk + 1],
                          hT[:, 1024 * kk + 512 * c:1024 * kk + 512 * (c + 1)],
                          kk == 0, kk == DT - 1,
                          waits=(p.fillw(bk) + [hT_dv]) if kk == 0 else [],
                          inc="PE" if kk == DT - 1 else None)
            ac = p.emit("scalar", lambda e, c=c, bk=bk, hb=hb:
                        e.activation(grow[0:1, hb + 512 * c:hb + 512 * (c + 1)], pb[bk][0:1, :],
                                     AF.Exp, bias=nbg[0:1, 0:1], scale=-1.0),
                        waits=[pe_f, st["grow_dv"], st["grow_sy"]], inc="AC")
            p.consumed(bk, ac)
        if half == 1:
            acg = p.v("AC")

            def _g(e):
                e.tensor_scalar_add(urow[0:1, :], grow[0:1, :], 1.0)
                e.reciprocal(urow[0:1, :], urow[0:1, :])
                return e.tensor_tensor(grow[0:1, :], grow[0:1, :], urow[0:1, :], ALU.mult)
            dvg = p.emit("vector", _g, waits=[acg], inc="DV")
            st["grow_dv"] = dvg
            p.emit("sync", lambda e: e.dma_start(grow[64:65, :], urow[0:1, :]),
                   waits=[dvg, st["grow_use"]], inc="GR", incn=16)
            gsy = p.emit("sync", lambda e: e.dma_start(urow[64:65, :], grow[0:1, :]),
                         inc="GR", incn=16)
            st["grow_sy"] = gsy
            return gsy
        return Z

    def emit_v8(seq):
        pe_f = Z
        for c in range(16):
            for hset in range(2):
                rhs = vre[:, 8 * hset:8 * (hset + 1), c, 0:64]
                pe_f = mm(pb[6 + hset][0:1, :], onescb[:, 0:1], rhs, c == 0, c == 15,
                          waits=(p.fillw(6 + hset) + [st["vtok"]]) if c == 0 else [],
                          inc="PE" if (c == 15 and hset == 1) else None)

        def _v8(e):
            e.tensor_scalar_mul(v8[0:1, 0:512], pb[6][0:1, :], 1.0e-8)
            return e.tensor_scalar_mul(v8[0:1, 512:1024], pb[7][0:1, :], 1.0e-8)
        dv = p.emit("vector", _v8, waits=[pe_f, st["v8_sy"]], inc="DV")
        p.consumed(6, dv)
        p.consumed(7, dv)
        sys_ = p.emit("sync", lambda e: e.dma_start(v8scr, v8[0:1, 0:1024]),
                      waits=[dv, st["v8_use"]], inc="V8", incn=16)
        syc = sys_
        for h in range(16):
            syc = p.emit("sync", lambda e, h=h: e.dma_start(
                         v8c[0:64, h:h + 1],
                         v8scr[:, 64 * h:64 * h + 64].rearrange("p (d one) -> (p d) one", one=1)),
                         waits=[syc], inc="V8", incn=16)
        st["v8_sy"] = syc
        return syc

    def emit_proj(seq, qg, is_a, o_dvs, xsrc, xdst, sub):
        phase("PROJ")
        q0 = 512 * qg
        lxs = {}

        def xload(dt):
            par = dt % 2
            lxs[dt] = p.emit("sync", lambda e, dt=dt, par=par, xsrc=xsrc:
                             e.dma_start(xc[:, 512 * par:512 * (par + 1)],
                                         xsrc[seq, :, T * dt + q0:T * dt + q0 + 512]),
                             waits=[st["xc"][par]] + st["xw"], inc="XC%d" % par, incn=16)
        xload(0)
        xload(1)
        for dt in range(DT):
            bk = dt % 4
            pe_f = Z
            for kk in range(DT):
                pe_f = mm(pb[bk], wout_sb[:, 1024 * kk + 128 * dt:1024 * kk + 128 * (dt + 1)],
                          o_qg[:, 512 * kk:512 * (kk + 1)],
                          kk == 0, kk == DT - 1,
                          waits=(p.fillw(bk) + [o_dvs[kk], st["wout"]]) if kk == 0 else [o_dvs[kk]],
                          inc="PE" if kk == DT - 1 else None)
            par = dt % 2
            lx = lxs[dt]
            if is_a:
                dv = p.emit("vector", lambda e, bk=bk, dt=dt, par=par:
                            e.scalar_tensor_tensor(xc[:, 512 * par:512 * (par + 1)], pb[bk],
                                                   bo_pp[:, dt:dt + 1], xc[:, 512 * par:512 * (par + 1)],
                                                   ALU.add, ALU.add),
                            waits=[pe_f, lx], inc="DV")
            else:
                dv = p.emit("vector", lambda e, bk=bk, par=par:
                            e.tensor_tensor(xc[:, 512 * par:512 * (par + 1)], pb[bk],
                                            xc[:, 512 * par:512 * (par + 1)], ALU.add),
                            waits=[pe_f, lx], inc="DV")
            p.consumed(bk, dv)
            sy = p.emit("sync", lambda e, dt=dt, par=par, xdst=xdst:
                        e.dma_start(xdst[seq, :, T * dt + q0:T * dt + q0 + 512],
                                    xc[:, 512 * par:512 * (par + 1)]),
                        waits=[dv], inc="XS%d" % par, incn=16)
            if xdump is not None:
                sy = p.emit("sync", lambda e, dt=dt, par=par, sub=sub:
                            e.dma_start(xdump[sub, seq, :, T * dt + q0:T * dt + q0 + 512],
                                        xc[:, 512 * par:512 * (par + 1)]),
                            inc="XS%d" % par, incn=16)
            st["xc"][par] = sy
            if dt + 2 < DT:
                xload(dt + 2)
        st["o_qg_r"] = p.v("PE")

    def emit_attn_a(seq, gsy, xsrc, xdst, sub):
        vdv = st["vtok"]
        for qg in range(4):
            q0 = 512 * qg

            def maxpass(g):
                phase("AMAX")
                dv_m = Z
                for qt in range(4):
                    qi = 4 * qg + qt
                    w0 = max(0, 128 * qi - 256)
                    w1 = min(T, 128 * qi + 384)
                    wlen = w1 - w0
                    off_id = {0: 0, 128: 1, 256: 2}[128 * qi - w0]
                    rot = (qt % 2) * 2048
                    for hh in range(2):
                        pbase = 1024 * hh + rot
                        qsl = qT[64 * hh:64 * (hh + 1), T * g + 128 * qi:T * g + 128 * (qi + 1)]
                        pe_q = Z
                        c0 = 0
                        while c0 < wlen:
                            cw = min(512, wlen - c0)
                            pe_q = mm(pball[:, pbase + c0:pbase + c0 + cw], qsl,
                                      kT[64 * hh:64 * (hh + 1), T * g + w0 + c0:T * g + w0 + c0 + cw],
                                      True, True,
                                      waits=p.fillw(pbase // 512) + (p.fillw(pbase // 512 + 1) if c0 else []) + [st["qkT"]],
                                      tp=(64 * hh, 0), inc="PE")
                            c0 += cw
                        pe_q = mm(pball[:, pbase + 640:pbase + 704], qsl,
                                  kT[64 * hh:64 * (hh + 1), T * g:T * g + 64],
                                  True, True,
                                  waits=p.fillw(pbase // 512 + 1) if wlen <= 512 else [],
                                  tp=(64 * hh, 0), inc="PE")
                        if off_id == 2 and wlen == 640:
                            p.emit("vector", lambda e, pbase=pbase:
                                   e.tensor_tensor(pball[:, pbase:pbase + 128], pball[:, pbase:pbase + 128],
                                                   bandf[:, 1280:1408], ALU.add),
                                   waits=[pe_q], inc="DV")
                            dvd = p.emit("vector", lambda e, pbase=pbase:
                                         e.tensor_tensor(pball[:, pbase + 512:pbase + 640], pball[:, pbase + 512:pbase + 640],
                                                         bandf[:, 1792:1920], ALU.add),
                                         inc="DV")
                        else:
                            dvd = p.emit("vector", lambda e, pbase=pbase, wlen=wlen, off_id=off_id:
                                         e.tensor_tensor(pball[:, pbase:pbase + wlen], pball[:, pbase:pbase + wlen],
                                                         bandf[:, 640 * off_id:640 * off_id + wlen], ALU.add),
                                         waits=[pe_q], inc="DV")
                        dv_m = p.emit("vector", lambda e, pbase=pbase, wlen=wlen, hh=hh, qt=qt:
                                      e.reduce_max(m4[:, 4 * hh + qt:4 * hh + qt + 1],
                                                   pball[:, pbase:pbase + wlen], axis=AX.X),
                                      waits=[dvd, st["m4"]], inc="DV")
                        dv_m = p.emit("vector", lambda e, pbase=pbase, hh=hh, qt=qt:
                                      e.reduce_max(m4[:, 8 + 4 * hh + qt:8 + 4 * hh + qt + 1],
                                                   pball[:, pbase + 640:pbase + 704], axis=AX.X),
                                      inc="DV")
                        p.consumed(pbase // 512, dv_m)
                        p.consumed(pbase // 512 + 1, dv_m)
                pe_t = mm(pb[7][0:16, 0:128], m4[:, 0:16], i128[:, 0:128], True, True,
                          waits=p.fillw(7) + [dv_m], inc="PE")
                st["m4"] = pe_t
                dvt = p.emit("vector", lambda e: e.tensor_scalar_mul(stga[32:48, 0:128], pb[7][0:16, 0:128], -1.0),
                             waits=[pe_t, st["wrow_stage"][0], st["wrow_stage"][1]], inc="DV")
                p.consumed(7, dvt)
                sy2 = p.emit("gpsimd", lambda e: e.dma_start(nmrow[64:65, 0:2048], stga[32:48, 0:128]),
                             waits=[dvt, st["nm_use"]], inc="NM", incn=16)
                st["wrow_stage"][0] = sy2
                return sy2

            nms = [None] * 8
            nms[0] = maxpass(0)
            o_dvs = [Z] * 8
            for g in range(8):
                sy2 = nms[g]
                phase("AGLB")
                pe_g = Z
                for hh in range(2):
                    mm(pb[hh][0:64, :], kT[64 * hh:64 * (hh + 1), T * g:T * g + 64],
                       qT[64 * hh:64 * (hh + 1), T * g + q0:T * g + q0 + 512],
                       True, False, waits=p.fillw(hh) + [st["qkT"]],
                       tp=(64 * hh, 0))
                    pe_g = mm(pb[hh][0:64, :], onesbr[64:65, 0:64],
                              nmrow[64:65, 1024 + 512 * hh:1024 + 512 * (hh + 1)],
                              False, True, waits=[sy2], tp=(64, 0), inc="PE" if hh == 1 else None)
                ac_g = []
                for hh in range(2):
                    sl = 2 + hh
                    ac = p.emit("scalar", lambda e, hh=hh, sl=sl:
                                e.activation(ebuf[0:64, 512 * sl:512 * (sl + 1)], pb[hh][0:64, :], AF.Exp),
                                waits=[pe_g, st["ebuf"][sl]], inc="AC")
                    ac_g.append(ac)
                    p.consumed(hh, ac)
                pe_pvg = Z
                for hh in range(2):
                    h = 2 * g + hh
                    pe_pvg = mm(pb[6 + hh][0:65, :], vtok[0:64, (h * 16) * 65:(h * 16) * 65 + 65],
                                ebuf[0:64, 512 * (2 + hh):512 * (2 + hh) + 512],
                                True, True,
                                waits=p.fillw(6 + hh) + [ac_g[hh], vdv],
                                inc="PE" if hh == 1 else None)
                st["ebuf"][2] = pe_pvg
                st["ebuf"][3] = pe_pvg
                phase("ALOC")
                kcs = list(range(max(0, 4 * qg - 2), min(16, 4 * qg + 6)))
                pv_last = Z
                pq = []
                for i, kc in enumerate(kcs):
                    sA = 2 * (i % 3)
                    bA = 2 * (i % 2)
                    pe_st = Z
                    for hh in range(2):
                        bk = bA + hh
                        mm(pb[bk], kT[64 * hh:64 * (hh + 1), T * g + 128 * kc:T * g + 128 * (kc + 1)],
                           qT[64 * hh:64 * (hh + 1), T * g + q0:T * g + q0 + 512],
                           True, False, waits=p.fillw(bk), tp=(64 * hh, 0))
                        pe_st = mm(pb[bk], onesbr[64:65, 0:128],
                                   nmrow[64:65, 512 * hh:512 * (hh + 1)],
                                   False, True, waits=[sy2] if i == 0 else [], tp=(64, 0),
                                   inc="PE" if hh == 1 else None)
                    r = kc - 4 * qg + 2
                    dvms = []
                    for hh in range(2):
                        bk = bA + hh
                        sl = sA + hh
                        ac = p.emit("scalar", lambda e, bk=bk, sl=sl:
                                    e.activation(ebuf[:, 512 * sl:512 * (sl + 1)], pb[bk], AF.Exp),
                                    waits=[pe_st, st["ebuf"][sl]], inc="AC")
                        p.consumed(bk, ac)
                        dvm = p.emit("gpsimd", lambda e, sl=sl, r=r:
                                     e.tensor_tensor(ebuf[:, 512 * sl:512 * (sl + 1)],
                                                     ebuf[:, 512 * sl:512 * (sl + 1)],
                                                     band[:, 512 * r:512 * (r + 1)], ALU.mult),
                                     waits=[ac], inc="GP")
                        dvms.append(dvm)
                    def flush(last, kcs=kcs):
                        nonlocal pv_last
                        (pkc, psA, pdvms) = pq.pop(0)
                        for hh in range(2):
                            h = 2 * g + hh
                            sl = psA + hh
                            pv_last = mm(pb[4 + hh][0:65, :],
                                         vtok[:, (h * 16 + pkc) * 65:(h * 16 + pkc) * 65 + 65],
                                         ebuf[:, 512 * sl:512 * (sl + 1)],
                                         pkc == kcs[0], last,
                                         waits=[pdvms[hh]] + ((p.fillw(4 + hh) + [vdv]) if pkc == kcs[0] else []),
                                         inc="PE")
                            st["ebuf"][sl] = pv_last
                    pq.append((kc, sA, dvms))
                    if len(pq) > 2:
                        flush(False)
                while pq:
                    flush(len(pq) == 1)
                pe_pv = pv_last
                st["nm_use"] = pe_pv
                evl = []
                evg = []
                for hh in range(2):
                    ev = p.emit("scalar", lambda e, hh=hh:
                                e.activation(nla[0:65, 512 * hh:512 * (hh + 1)], pb[4 + hh][0:65, :], AF.Copy),
                                waits=[pe_pv, st["nla"]], inc="AC")
                    p.consumed(4 + hh, ev)
                    evl.append(ev)
                    ev2 = p.emit("scalar", lambda e, hh=hh:
                                 e.activation(nga[0:65, 512 * hh:512 * (hh + 1)], pb[6 + hh][0:65, :], AF.Copy),
                                 inc="AC")
                    p.consumed(6 + hh, ev2)
                    evg.append(ev2)
                if g + 1 < 8:
                    nms[g + 1] = maxpass(g + 1)
                phase("BEPI")
                for hh in range(2):
                    def _rows1(e, hh=hh, q0=q0):
                        e.reciprocal(erow[0][64:65, :], nla[64:65, 512 * hh:512 * hh + 512])
                        e.reciprocal(erow[1][64:65, :], nga[64:65, 512 * hh:512 * hh + 512])
                        return e.tensor_tensor(erow[2][64:65, :], urow[64:65, q0:q0 + 512], erow[0][64:65, :], ALU.mult)
                    dvr1 = p.emit("vector", _rows1,
                                  waits=[evl[hh], evg[hh], gsy, st["erow"], st["erow_pe"]], inc="DV")
                    dvr = p.emit("vector", lambda e, q0=q0:
                                 e.tensor_tensor(erow[0][64:65, :], grow[64:65, q0:q0 + 512],
                                                 erow[1][64:65, :], ALU.mult), inc="DV")
                    st["erow"] = dvr
                    mm(pb[4][0:64, :], onesr[64:65, 0:64],
                       erow[2][64:65, :], True, True,
                       waits=p.fillw(4) + [dvr1], tp=(64, 0))
                    peb = mm(pb[5][0:64, :], onesr[64:65, 0:64],
                             erow[0][64:65, :], True, True,
                             waits=p.fillw(5) + [dvr], tp=(64, 0), inc="PE")
                    st["erow_pe"] = peb

                    def _blk(e, hh=hh, g=g):
                        e.tensor_tensor(epi[0:64, 0:512], nla[0:64, 512 * hh:512 * hh + 512],
                                        pb[4][0:64, :], ALU.mult)
                        e.tensor_tensor(epi[0:64, 512:1024], nga[0:64, 512 * hh:512 * hh + 512],
                                        pb[5][0:64, :], ALU.mult)
                        return e.tensor_tensor(o_qg[64 * hh:64 * (hh + 1), 512 * g:512 * (g + 1)],
                                               epi[0:64, 0:512], epi[0:64, 512:1024], ALU.add)
                    dvb = p.emit("vector", _blk, waits=[peb, st["epi"], st["o_qg_w"]], inc="DV")
                    st["epi"] = dvb
                    for sl in range(4):
                        st["ebuf"][sl] = dvb
                    for bk in (4, 5):
                        p.consumed(bk, dvb)
                    st["nla"] = dvb
                    o_dvs[g] = dvb
            emit_proj(seq, qg, True, o_dvs, xsrc, xdst, sub)
            st["o_qg_w"] = st["o_qg_r"]
        st["grow_use"] = st["erow"]

    C1 = 1.0 + T * 1.0e-8
    C2 = (T * 1.0e-8) * 1.0e-6

    def emit_attn_b(seq, v8sy, xsrc, xdst, sub):
        vdv = st["vtok"]
        mrows = (mrow, mrow2)
        wrows = (wrow, wrow2)
        for qg in range(4):
            q0 = 512 * qg

            def maxpass(g):
                par = g % 2
                phase("BMAX")
                dv_m = Z
                for qt in range(4):
                    qi = 4 * qg + qt
                    L = 128 * (qi + 1)
                    nch = (L + 511) // 512
                    rot = (qt % 2) * 2 if nch <= 2 else 0
                    pe_q = Z
                    for hh in range(2):
                        for kc5 in range(nch):
                            w = min(512, L - 512 * kc5)
                            bk = 4 * hh + rot + kc5
                            pe_q = mm(pb[bk][:, 0:w],
                                      qT[64 * hh:64 * (hh + 1), T * g + 128 * qi:T * g + 128 * (qi + 1)],
                                      kT[64 * hh:64 * (hh + 1), T * g + 512 * kc5:T * g + 512 * kc5 + w],
                                      True, True, waits=p.fillw(bk) + [st["qkT"]],
                                      tp=(64 * hh, 0),
                                      inc="PE" if (hh == 1 and kc5 == nch - 1) else None)
                    for hh in range(2):
                        lastb = 4 * hh + rot + nch - 1
                        wq = min(512, L - 512 * (nch - 1))
                        dvd = p.emit("vector", lambda e, lastb=lastb, wq=wq:
                                     e.tensor_tensor(pb[lastb][:, wq - 128:wq], pb[lastb][:, wq - 128:wq],
                                                     qmaj[:, 0:128], ALU.add),
                                     waits=[pe_q], inc="DV")
                        dv_m = p.emit("vector", lambda e, hh=hh, qt=qt, L=L, rot=rot:
                                      e.reduce_max(m4[:, 4 * hh + qt:4 * hh + qt + 1],
                                                   pball[:, 2048 * hh + 512 * rot:2048 * hh + 512 * rot + L], axis=AX.X),
                                      waits=[dvd, st["m4"]], inc="DV")
                        for kc5 in range(nch):
                            p.consumed(4 * hh + rot + kc5, dv_m)
                pe_t = mm(pb[7][0:8, 0:128], m4[:, 0:8], i128[:, 0:128],
                          True, True, waits=p.fillw(7) + [dv_m], inc="PE")
                st["m4"] = pe_t
                wst = wrows[par]
                dvt = p.emit("vector", lambda e, wst=wst: e.tensor_copy(wst[0:8, 0:128], pb[7][0:8, 0:128]),
                             waits=[pe_t, st["wrow_stage"][par]], inc="DV")
                p.consumed(7, dvt)
                sy2 = p.emit("sync", lambda e, par=par, wst=wst:
                             e.dma_start(mrows[par][64:65, 0:1024], wst[0:8, 0:128]),
                             waits=[dvt, st["mrow"][par]], inc="MR", incn=16)
                st["wrow_stage"][par] = sy2
                acw = p.emit("scalar", lambda e, par=par:
                             e.activation(wrows[par][64:65, 0:1024], mrows[par][64:65, 0:1024],
                                          AF.Exp, scale=-1.0),
                             waits=[sy2, st["wrow64"][par]], inc="AC")
                st["mrow"][par] = acw
                return acw

            acws = [None] * 8
            acws[0] = maxpass(0)
            o_dvs = [Z] * 8
            for g in range(8):
                par = g % 2
                phase("BSTPV")
                nkc = 4 * (qg + 1)
                pv_last = Z
                pq = []
                for kc in range(nkc):
                    sA = 2 * (kc % 3)
                    bA = 2 * (kc % 2)
                    pe_st = Z
                    for hh in range(2):
                        bk = bA + hh
                        pe_st = mm(pb[bk], kT[64 * hh:64 * (hh + 1), T * g + 128 * kc:T * g + 128 * (kc + 1)],
                                   qT[64 * hh:64 * (hh + 1), T * g + q0:T * g + q0 + 512],
                                   True, True, waits=p.fillw(bk), tp=(64 * hh, 0),
                                   inc="PE" if hh == 1 else None)
                    diag = kc - 4 * qg
                    wms = []
                    for hh in range(2):
                        bk = bA + hh
                        sl = sA + hh
                        ac = p.emit("scalar", lambda e, bk=bk, sl=sl:
                                    e.activation(ebuf[:, 512 * sl:512 * (sl + 1)], pb[bk], AF.Exp),
                                    waits=[pe_st, st["ebuf"][sl]], inc="AC")
                        p.consumed(bk, ac)
                        wm = ac
                        if diag >= 0:
                            wm = p.emit("gpsimd", lambda e, sl=sl, diag=diag:
                                        e.tensor_tensor(ebuf[:, 512 * sl:512 * (sl + 1)],
                                                        ebuf[:, 512 * sl:512 * (sl + 1)],
                                                        cst[:, 512 * diag:512 * (diag + 1)], ALU.mult),
                                        waits=[ac], inc="GP")
                        wms.append(wm)
                    def flushb(last, nkc=nkc):
                        nonlocal pv_last
                        (pkc, psA, pwms) = pq.pop(0)
                        for hh in range(2):
                            h = 2 * g + hh
                            sl = psA + hh
                            pv_last = mm(pb[4 + hh][0:65, :],
                                         vtok[:, (h * 16 + pkc) * 65:(h * 16 + pkc) * 65 + 65],
                                         ebuf[:, 512 * sl:512 * (sl + 1)],
                                         pkc == 0, last,
                                         waits=[pwms[hh]] + ((p.fillw(4 + hh) + [vdv]) if pkc == 0 else []),
                                         inc="PE")
                            st["ebuf"][sl] = pv_last
                    pq.append((kc, sA, wms))
                    if len(pq) > 2:
                        flushb(False)
                while pq:
                    flushb(len(pq) == 1)
                pe_pv = pv_last
                evs = []
                for hh in range(2):
                    ev = p.emit("scalar", lambda e, hh=hh:
                                e.activation(ngb[0:65, 512 * hh:512 * (hh + 1)], pb[4 + hh][0:65, :], AF.Copy),
                                waits=[pe_pv, st["ngb"]], inc="AC")
                    p.consumed(4 + hh, ev)
                    evs.append(ev)
                if g + 1 < 8:
                    acws[g + 1] = maxpass(g + 1)
                phase("BEPI")
                acw = acws[g]
                wr = wrows[par]
                for hh in range(2):
                    h = 2 * g + hh

                    def _rows1(e, hh=hh, wr=wr):
                        e.tensor_tensor(erow[0][64:65, :], ngb[64:65, 512 * hh:512 * hh + 512],
                                        wr[64:65, 512 * hh:512 * hh + 512], ALU.mult)
                        e.tensor_scalar(erow[1][64:65, :], erow[0][64:65, :], C1, C2, ALU.mult, ALU.add)
                        e.reciprocal(erow[1][64:65, :], erow[1][64:65, :])
                        return e.tensor_tensor(erow[2][64:65, :], wr[64:65, 512 * hh:512 * hh + 512],
                                               erow[1][64:65, :], ALU.mult)
                    dvr1 = p.emit("vector", _rows1, waits=[evs[hh], acw, st["erow"], st["erow_pe"]], inc="DV")
                    dvr = p.emit("vector", lambda e:
                                 e.scalar_tensor_tensor(erow[0][64:65, :], erow[0][64:65, :], 1.0e-6,
                                                        erow[1][64:65, :], ALU.add, ALU.mult), inc="DV")
                    st["erow"] = dvr
                    st["wrow64"][par] = dvr
                    mm(pb[6][0:64, :], onesr[64:65, 0:64],
                       erow[2][64:65, :], True, True,
                       waits=p.fillw(6) + [dvr1], tp=(64, 0))
                    peb = mm(pb[7][0:64, :], onesr[64:65, 0:64],
                             erow[0][64:65, :], True, True,
                             waits=p.fillw(7) + [dvr], tp=(64, 0), inc="PE")
                    st["erow_pe"] = peb

                    def _blk(e, hh=hh, h=h, g=g):
                        e.tensor_tensor(epi[0:64, 0:512], ngb[0:64, 512 * hh:512 * hh + 512],
                                        pb[6][0:64, :], ALU.mult)
                        return e.scalar_tensor_tensor(o_qg[64 * hh:64 * (hh + 1), 512 * g:512 * (g + 1)],
                                                      pb[7][0:64, :], v8c[0:64, h:h + 1],
                                                      epi[0:64, 0:512], ALU.mult, ALU.add)
                    dvb = p.emit("vector", _blk, waits=[peb, v8sy, st["epi"], st["o_qg_w"]], inc="DV")
                    st["epi"] = dvb
                    st["ebuf"][0] = dvb
                    st["ebuf"][1] = dvb
                    for bk in (6, 7):
                        p.consumed(bk, dvb)
                    st["ngb"] = dvb
                    st["v8_use"] = dvb
                    o_dvs[g] = dvb
            emit_proj(seq, qg, False, o_dvs, xsrc, xdst, sub)
            st["o_qg_w"] = st["o_qg_r"]

    def emit_ffn(seq, xsrc, xdst, sub):
        for half in range(2):
            hb = half * 1024
            h_dv0, h_dv = emit_ln(seq, half, xsrc, ffn_mode=True)
            if paranoid >= 1:
                joinall()
            phase("FL0")
            for sc in range(8):
                par = sc % 2
                lw = Z
                for g2 in range(2):
                    lw = p.emit("sync", lambda e, sc=sc, par=par, g2=g2:
                                e.dma_start(w0b[:, 4096 * par + 2048 * g2:4096 * par + 2048 * (g2 + 1)]
                                            .rearrange("p (k c) -> p k c", k=4),
                                            w0s[4 * g2:4 * (g2 + 1), :, 512 * sc:512 * (sc + 1)]
                                            .rearrange("k p c -> p k c")),
                                waits=[st["w0b"][par], st["conv"]] if g2 == 0 else [], inc="W0%d" % par, incn=16)
                for tc in range(2):
                    for dp in range(2):
                        for dd in range(2):
                            dt_sc = 2 * dp + dd
                            ft = 4 * sc + dt_sc
                            bk = 4 * dp + 2 * dd + tc
                            pe_f = Z
                            for kk in range(DT):
                                pe_f = mm(pb[bk],
                                          w0b[:, 4096 * par + 512 * kk + 128 * dt_sc:4096 * par + 512 * kk + 128 * (dt_sc + 1)],
                                          h0T[:, 1024 * kk + 512 * tc:1024 * kk + 512 * (tc + 1)],
                                          kk == 0, kk == DT - 1,
                                          waits=(p.fillw(bk) + [lw, h_dv0 if tc == 0 else h_dv]) if kk == 0 else [],
                                          inc="PE" if kk == DT - 1 else None)
                            ac = p.emit("scalar", lambda e, bk=bk, ft=ft, tc=tc:
                                        e.activation(a1T[:, 1024 * ft + 512 * tc:1024 * ft + 512 * (tc + 1)],
                                                     pb[bk], AF.Relu, scale=sc01[:, 0:1]),
                                        waits=[pe_f], inc="AC")
                            p.consumed(bk, ac)
                st["w0b"][par] = p.v("PE")
            ac_a1 = p.v("AC")
            if paranoid >= 1:
                joinall()
            phase("FL1")
            for s in range(16):
                par = s % 2
                lw = Z
                for g2 in range(4):
                    lw = p.emit("sync", lambda e, s=s, par=par, g2=g2:
                                e.dma_start(w1b[:, 8192 * par + 2048 * g2:8192 * par + 2048 * (g2 + 1)]
                                            .rearrange("p (k c) -> p k c", k=8),
                                            w1s[s // 2, 8 * g2:8 * (g2 + 1), :, 256 * (s % 2):256 * (s % 2) + 256]
                                            .rearrange("k p c -> p k c")),
                                waits=[st["w1b"][par], st["conv"]] if g2 == 0 else [], inc="W1%d" % par, incn=16)
                if debug_stage == "ffn2" and s == 0 and half == 0:
                    p.emit("gpsimd", lambda e: e.dma_start(dbg[0, :, 0:8192], w1b[:, 0:8192]), waits=[lw], inc="XD", incn=16)
                    p.emit("gpsimd", lambda e: e.dma_start(dbg[3, 0:1, 8190:8198], srow[0:1, 0:8]), inc="XD", incn=16)
                    p.emit("gpsimd", lambda e: e.dma_start(dbg[3, 1:2, 0:128], sc01[:, 0:1]), inc="XD", incn=16)
                    p.emit("gpsimd", lambda e: e.dma_start(dbg[3, 2:3, 0:128], sc12[:, 0:1]), inc="XD", incn=16)
                for dd in range(2):
                    for tc in range(2):
                        bk = 4 * par + 2 * dd + tc
                        pe_f = Z
                        for kk in range(FT):
                            pe_f = mm(pb[bk],
                                      w1b[:, 8192 * par + 256 * kk + 128 * dd:8192 * par + 256 * kk + 128 * (dd + 1)],
                                      a1T[:, 1024 * kk + 512 * tc:1024 * kk + 512 * (tc + 1)],
                                      kk == 0, kk == FT - 1,
                                      waits=(p.fillw(bk) + [lw, ac_a1]) if kk == 0 else [],
                                      inc="PE" if kk == FT - 1 else None)
                        ac = p.emit("scalar", lambda e, bk=bk, par=par, dd=dd, tc=tc:
                                    e.activation(a2s[:, 2048 * par + 1024 * dd + 512 * tc:2048 * par + 1024 * dd + 512 * (tc + 1)],
                                                 pb[bk], AF.Relu, scale=sc12[:, 0:1]),
                                    waits=[pe_f, st["a2s"][par]], inc="AC")
                        p.consumed(bk, ac)
                st["w1b"][par] = p.v("PE")
                ac_a2 = p.v("AC")
                if debug_stage == "ffn2" and s == 1 and half == 0:
                    p.emit("gpsimd", lambda e: e.dma_start(dbg[1, :, 0:4096], a2s[:, 0:4096]), waits=[ac_a2], inc="XD", incn=16)
                if s % 2 == 1:
                    fs = s // 2
                    lw2 = p.emit("sync", lambda e, fs=fs:
                                 e.dma_start(w2b[:, :].rearrange("p (k c) -> p k c", k=4),
                                             w2s[4 * fs:4 * (fs + 1), :, :].rearrange("k p c -> p k c")),
                                 waits=[st["w2b"], st["conv"]], inc="W2", incn=16)
                    if debug_stage == "ffn2" and s == 1 and half == 0:
                        p.emit("gpsimd", lambda e: e.dma_start(dbg[2, :, 0:4096], w2b[:, 0:4096]), waits=[lw2], inc="XD", incn=16)
                    dv2 = Z
                    for tc in range(2):
                        for dt in range(DT):
                            bk = (2 * dt + tc) % 4
                            pe_f = Z
                            for k2 in range(4):
                                kpar, kd = k2 // 2, k2 % 2
                                pe_f = mm(pb[bk],
                                          w2b[:, 1024 * k2 + 128 * dt:1024 * k2 + 128 * (dt + 1)],
                                          a2s[:, 2048 * kpar + 1024 * kd + 512 * tc:2048 * kpar + 1024 * kd + 512 * (tc + 1)],
                                          k2 == 0, k2 == 3,
                                          waits=(p.fillw(bk) + [lw2, ac_a2]) if k2 == 0 else [],
                                          inc="PE" if k2 == 3 else None)
                            dv2 = p.emit("vector", lambda e, bk=bk, dt=dt, tc=tc:
                                         e.tensor_tensor(outh[:, 1024 * dt + 512 * tc:1024 * dt + 512 * (tc + 1)],
                                                         outh[:, 1024 * dt + 512 * tc:1024 * dt + 512 * (tc + 1)],
                                                         pb[bk], ALU.add),
                                         waits=[pe_f], inc="DV")
                            p.consumed(bk, dv2)
                    st["w2b"] = p.v("PE")
                    st["a2s"][0] = dv2
                    st["a2s"][1] = dv2
                    if paranoid >= 2:
                        joinall()
            if paranoid >= 1:
                joinall()
            dv_all = p.v("DV")
            sy_l = Z
            for dt in range(DT):
                sy_l = p.emit("sync", lambda e, dt=dt, hb=hb, xdst=xdst:
                              e.dma_start(xdst[seq, :, T * dt + hb:T * dt + hb + 1024],
                                          outh[:, 1024 * dt:1024 * (dt + 1)]),
                              waits=[dv_all] if dt == 0 else [], inc="XO", incn=16)
                if xdump is not None:
                    sy_l = p.emit("sync", lambda e, dt=dt, hb=hb, sub=sub:
                                  e.dma_start(xdump[sub, seq, :, T * dt + hb:T * dt + hb + 1024],
                                              outh[:, 1024 * dt:1024 * (dt + 1)]),
                                  inc="XO", incn=16)
            st["outh_sy"] = sy_l
            st["xw"] = [p.v("XO")]
            if debug_stage == "ffn2" and half == 0:
                p.emit("gpsimd", lambda e: e.dma_start(dbg[3, :, 0:8192], outh[:, 0:8192]), waits=[p.v("DV")], inc="XD", incn=16)
                return
            if debug_stage == "ffn" and half == 0:
                dq = p.v("AC")
                p.emit("gpsimd", lambda e: e.dma_start(dbg[0, :, 0:8192], h0T[:, :]), waits=[dq, p.v("DV")], inc="XD", incn=16)
                p.emit("gpsimd", lambda e: e.dma_start(dbg[1, :, 0:8192], a1T[:, 0:8192]), inc="XD", incn=16)
                p.emit("gpsimd", lambda e: e.dma_start(dbg[2, :, 0:8192], a1T[:, 24576:32768]), inc="XD", incn=16)
                p.emit("gpsimd", lambda e: e.dma_start(dbg[3, :, 0:8192], outh[:, 0:8192]), inc="XD", incn=16)
                return

    def emit_attn(seq, is_a, xsrc, xdst, sub):
        p.wait("gpsimd", [st["w1b"][0], st["w1b"][1], st["w2b"], st["w0b"][0], st["w0b"][1]])
        gsy = Z
        for half in range(2):
            hdv0, h_dv = emit_ln(seq, half, xsrc, ffn_mode=False)
            emit_qkv(seq, half, is_a, hdv0, h_dv)
            if is_a:
                gsy = emit_gate(seq, h_dv, half)
            st["hT_pe"] = p.v("PE")
        st["qkT"] = p.v("AC")
        wsrc = wout_a if is_a else wproj_b
        lwp = Z
        for kk in range(DT):
            lwp = p.emit("gpsimd", lambda e, kk=kk, wsrc=wsrc:
                         e.dma_start(wout_sb[:, 1024 * kk:1024 * (kk + 1)],
                                     wsrc[128 * kk:128 * (kk + 1), :]),
                         waits=[st["qkT"], st["wbufs"][0], st["wbufs"][1]] if kk == 0 else [], inc="WO", incn=16)
        st["wout"] = lwp
        if debug_stage == "attng":
            vdv = st["vtok"]
            for g in range(2):
                pe_g = Z
                for hh in range(2):
                    pe_g = mm(pb[hh][0:64, :], kT[64 * hh:64 * (hh + 1), T * g:T * g + 64],
                              qT[64 * hh:64 * (hh + 1), T * g:T * g + 512],
                              True, True, waits=p.fillw(hh) + [st["qkT"]],
                              tp=(64 * hh, 0), inc="PE" if hh == 1 else None)
                dvp = p.emit("vector", lambda e, g=g: e.tensor_copy(lnt[0:64, 0:512], pb[0][0:64, :]), waits=[pe_g], inc="DV")
                p.emit("gpsimd", lambda e, g=g: e.dma_start(dbg[0, 64 * g:64 * (g + 1), 0:512], lnt[0:64, 0:512]), waits=[dvp], inc="XD", incn=16)
                ac_g = []
                for hh in range(2):
                    sl = 2 + hh
                    ac = p.emit("scalar", lambda e, hh=hh, sl=sl:
                                e.activation(ebuf[0:64, 512 * sl:512 * (sl + 1)], pb[hh][0:64, :], AF.Exp),
                                waits=[pe_g, st["ebuf"][sl]], inc="AC")
                    ac_g.append(ac)
                    p.consumed(hh, ac)
                p.emit("gpsimd", lambda e, g=g: e.dma_start(dbg[1, 64 * g:64 * (g + 1), 0:1024], ebuf[0:64, 1024:2048]),
                       waits=[ac_g[1]], inc="XD", incn=16)
                for hh in range(2):
                    h = 2 * g + hh
                    pe_pvg = mm(pb[6 + hh][0:65, :], vtok[0:64, (h * 16) * 65:(h * 16) * 65 + 65],
                                ebuf[0:64, 512 * (2 + hh):512 * (2 + hh) + 512],
                                True, True, waits=p.fillw(6 + hh) + [ac_g[hh], vdv],
                                inc="PE" if hh == 1 else None)
                    st["ebuf"][2] = pe_pvg
                    st["ebuf"][3] = pe_pvg
                dvq = p.emit("vector", lambda e: e.tensor_copy(lnt[0:65, 0:512], pb[6][0:65, :]), waits=[pe_pvg], inc="DV")
                p.emit("gpsimd", lambda e, g=g: e.dma_start(dbg[2, 0:65, 512 * g:512 * (g + 1)], lnt[0:65, 0:512]), waits=[dvq], inc="XD", incn=16)
                p.consumed(6, dvq)
                p.consumed(7, dvq)
            return
        if debug_stage == "qkv":
            dq = p.v("DV")
            aq = p.v("AC")
            p.emit("gpsimd", lambda e: e.dma_start(dbg[0], qT[:, :]), waits=[dq, aq, gsy], inc="XD", incn=16)
            p.emit("gpsimd", lambda e: e.dma_start(dbg[1], kT[:, :]), inc="XD", incn=16)
            p.emit("gpsimd", lambda e: e.dma_start(dbg[2, :, 0:16384], vtok[:, 0:16384]), inc="XD", incn=16)
            p.emit("gpsimd", lambda e: e.dma_start(dbg[3, 64:65, 0:T], grow[64:65, :]), inc="XD", incn=16)
            p.emit("gpsimd", lambda e: e.dma_start(dbg[3, 64:65, T:2 * T], urow[64:65, :]), inc="XD", incn=16)
            return
        if is_a:
            emit_attn_a(seq, gsy, xsrc, xdst, sub)
        else:
            v8sy = emit_v8(seq)
            emit_attn_b(seq, v8sy, xsrc, xdst, sub)
        st["xw"] = [p.v("XS0"), p.v("XS1")]

    for seq in range(seqs):
        for sub in range(n_sub):
            kind = chain[sub]
            xsrc = xin if sub == 0 else xbuf
            xdst = y if sub == n_sub - 1 else xbuf
            if kind == "ffn":
                emit_ffn(seq, xsrc, xdst, sub)
            else:
                emit_attn(seq, kind == "attn_a", xsrc, xdst, sub)

    if dump_x:
        marks = [p.v(x) for x in ("PE", "DV", "AC") + DMAS]
        p.wait("gpsimd", marks)
        p.emit("gpsimd", lambda e: e.dma_start(enddump[0, :, 0:128], smallp[:, :].bitcast(F32)[:, 0:128]), inc="XD", incn=16)
        p.emit("gpsimd", lambda e: e.dma_start(enddump[1, 0:1, 0:128], sc01[:, 0:1]), inc="XD", incn=16)
        p.emit("gpsimd", lambda e: e.dma_start(enddump[2, 0:1, 0:128], sc12[:, 0:1]), inc="XD", incn=16)
        p.emit("gpsimd", lambda e: e.dma_start(enddump[3, :, 0:512], bv_b[:, :].bitcast(F32)[:, 0:512]), inc="XD", incn=16)
        p.emit("gpsimd", lambda e: e.dma_start(enddump[4, :, 0:128], wg_bf[:, :].bitcast(F32)[:, 0:128]), inc="XD", incn=16)
        p.emit("gpsimd", lambda e: e.dma_start(enddump[5, :, 0:1024], band[:, :].bitcast(F32)[:, 0:1024]), inc="XD", incn=16)
        p.emit("gpsimd", lambda e: e.dma_start(enddump[6, :, 0:512], bandf[:, :].bitcast(F32)[:, 0:512]), inc="XD", incn=16)
        p.emit("gpsimd", lambda e: e.dma_start(enddump[7, :, 0:128], onesr[:, :].bitcast(F32)[:, 0:32].bitcast(F32)) if False else e.dma_start(enddump[7, :, 0:128], onesr[:, 0:128]), inc="XD", incn=16)
    final = [p.v(x) for x in DMAS] + ([p.v("XD")] if dump_x else [])
    p.wait("sync", final)
    p.wait("gpsimd", final)

    with nc.Block() as block:
        @block.tensor
        def _(t):
            for f in p.ops["tensor"]:
                f(t)

        @block.vector
        def _(v):
            for f in p.ops["vector"]:
                f(v)

        @block.scalar
        def _(s):
            for f in p.ops["scalar"]:
                f(s)

        @block.gpsimd
        def _(g):
            for f in p.ops["gpsimd"]:
                f(g)

        @block.sync
        def _(s):
            for f in p.ops["sync"]:
                f(s)
    return nc



def make_inputs(inputs, seqs_per_core=4):
    x = np.ascontiguousarray(np.asarray(inputs["L0_attn_a_hidden_states"]).transpose(0, 2, 1))
    B = x.shape[0]
    k0 = np.asarray(inputs["L0_ffn_k0"])
    com = {
        "wqkv_a": np.ascontiguousarray(np.asarray(inputs["L0_attn_a_Wqkv"], np.float32)),
        "bqkv_a": np.ascontiguousarray(np.asarray(inputs["L0_attn_a_bqkv"], np.float32).reshape(3 * D, 1)),
        "wout_a": np.ascontiguousarray(np.asarray(inputs["L0_attn_a_Wout"], np.float32)),
        "bout_a": np.ascontiguousarray(np.asarray(inputs["L0_attn_a_bout"], np.float32).reshape(D, 1)),
        "wg_a": np.ascontiguousarray(np.asarray(inputs["L0_attn_a_Wg"], np.float32)),
        "bg_a": np.ascontiguousarray(np.asarray(inputs["L0_attn_a_bg"], np.float32).reshape(1, 1)),
        "wqkv_b": np.ascontiguousarray(np.asarray(inputs["L0_attn_b_w_qkv"], np.float32).T),
        "wproj_b": np.ascontiguousarray(np.asarray(inputs["L0_attn_b_w_proj"], np.float32).T),
        "k0a": np.ascontiguousarray(k0[:D].astype(np.uint8)),
        "k0b": np.ascontiguousarray(k0[D:].astype(np.uint8)),
        "k1": np.ascontiguousarray(np.asarray(inputs["L0_ffn_k1"]).astype(np.uint8)),
        "k2": np.ascontiguousarray(np.asarray(inputs["L0_ffn_k2"]).astype(np.uint8)),
        "wg8": np.ascontiguousarray(np.asarray(inputs["L0_attn_a_Wg"], np.float32).reshape(8, 128).T),
        "s0": np.asarray(inputs["L0_ffn_s0"], np.float32).reshape(1, 1),
        "s1": np.asarray(inputs["L0_ffn_s1"], np.float32).reshape(1, 1),
        "s2": np.asarray(inputs["L0_ffn_s2"], np.float32).reshape(1, 1),
    }
    sp = np.zeros((128, 32), np.float32)
    sp[:, 0:8] = np.asarray(inputs["L0_attn_a_bqkv"], np.float32)[0:1024].reshape(8, 128).T
    sp[:, 8:16] = np.asarray(inputs["L0_attn_a_bqkv"], np.float32)[1024:2048].reshape(8, 128).T
    sp[:, 16:24] = np.asarray(inputs["L0_attn_a_bout"], np.float32).reshape(8, 128).T
    sp[0, 24] = np.float32(np.asarray(inputs["L0_ffn_s0"]).reshape(()))
    sp[0, 25] = np.float32(np.asarray(inputs["L0_ffn_s1"]).reshape(()))
    sp[0, 26] = np.float32(np.asarray(inputs["L0_ffn_s2"]).reshape(()))
    sp[0, 27] = np.float32(np.asarray(inputs["L0_attn_a_bg"]).reshape(()))
    com["smallpack"] = np.ascontiguousarray(sp)
    com.update(host_constants())
    ncores = B // seqs_per_core
    in_maps = []
    for c in range(ncores):
        im = dict(com)
        xs = x[c * seqs_per_core:(c + 1) * seqs_per_core]
        im["x"] = np.ascontiguousarray(
            xs.reshape(seqs_per_core, DT, 128, T).transpose(0, 2, 1, 3).reshape(seqs_per_core, 128, DT * T))
        in_maps.append(im)
    return in_maps, ncores


def unpack_y(res_list, seqs_per_core=4):
    outs = []
    for r in res_list:
        yv = np.asarray(r["y"])
        yv = yv.reshape(seqs_per_core, 128, DT, T).transpose(0, 2, 1, 3).reshape(seqs_per_core, D, T)
        outs.append(yv.transpose(0, 2, 1))
    return np.concatenate(outs, axis=0).astype(np.float32)


def kernel(**inputs):
    in_maps, ncores = make_inputs(inputs)
    nc = bass.Bass("TRN2", target_bir_lowering=False, detect_race_conditions=False)
    build_core(nc, n_sub=4, seqs=4)
    res = run_bass_kernel_spmd(nc, in_maps, core_ids=list(range(ncores)))
    return unpack_y(res.results)


# version 14
# speedup vs baseline: 1.0178x; 1.0178x over previous
import sys
import numpy as np
import ml_dtypes
import concourse.bass as bass
import concourse.mybir as mybir
from concourse.bass_utils import run_bass_kernel_spmd

F32 = mybir.dt.float32
F32R = mybir.dt.float32r
BF16 = mybir.dt.bfloat16
U8 = mybir.dt.uint8
AF = mybir.ActivationFunctionType
ALU = mybir.AluOpType
AX = mybir.AxisListType

T = 2048
D = 1024
DT = 8
H = 16
F1 = 4096
FT = 32
NEG = -1.0e30
KB = 1024
Z = ("SY", 0)


class P:
    def __init__(self, nc):
        self.nc = nc
        self.ops = {e: [] for e in ("tensor", "vector", "scalar", "gpsimd", "sync")}
        self.sems = {}
        self.cnt = {}
        self.bankw = [tuple() for _ in range(8)]
        self.hwm = {e: {} for e in self.ops}
        self.contrib = {}

    def sem(self, name):
        self.sems[name] = self.nc.alloc_semaphore(name=name)
        self.cnt[name] = 0
        self.contrib[name] = {}

    def _filter(self, eng, waits):
        best = {}
        for s, v in waits:
            if v > 0 and v > best.get(s, 0):
                best[s] = v
        hw = self.hwm[eng]
        semrefs = []
        for s, v in best.items():
            if v <= hw.get(s, 0):
                continue
            hw[s] = v
            if v <= self.contrib[s].get(eng, 0):
                continue
            semrefs.append((self.sems[s], v))
        return tuple(semrefs)

    def emit(self, eng, build, waits=(), inc=None, incn=1):
        semrefs = self._filter(eng, waits)
        inch = self.sems[inc] if inc else None

        def f(e, _b=build, _w=semrefs, _i=inch, _n=incn):
            for h, v in _w:
                e.wait_ge(h, v)
            ins = _b(e)
            if _i is not None:
                ins.then_inc(_i, _n)
        self.ops[eng].append(f)
        if inc:
            self.cnt[inc] += incn
            if incn == 1:
                self.contrib[inc][eng] = self.contrib[inc].get(eng, 0) + 1
            return (inc, self.cnt[inc])
        return None

    def wait(self, eng, waits):
        semrefs = self._filter(eng, waits)

        def f(e, _w=semrefs):
            for h, v in _w:
                e.wait_ge(h, v)
        self.ops[eng].append(f)

    def v(self, name):
        return (name, self.cnt[name])

    def fillw(self, bk, extra=()):
        w = list(self.bankw[bk]) + list(extra)
        self.bankw[bk] = tuple()
        return w

    def consumed(self, bk, mark):
        self.bankw[bk] = (mark,)


def host_constants():
    c = {}
    j = np.arange(128)[:, None]
    q = np.arange(512)[None, :]
    bm = np.zeros((8, 128, 512), dtype=np.uint8)
    for r in range(8):
        delta = -256 + 128 * r
        bm[r] = (np.abs(delta + j - q) <= 256).astype(np.uint8)
    c["band_masks"] = np.ascontiguousarray(bm.transpose(1, 0, 2).reshape(128, 8 * 512))
    cm = np.zeros((4, 128, 512), dtype=np.uint8)
    for r in range(4):
        cm[r] = ((128 * r + j) <= q).astype(np.uint8)
    c["cst_masks"] = np.ascontiguousarray(cm.transpose(1, 0, 2).reshape(128, 4 * 512))
    qq = np.arange(128)[:, None]
    jj = np.arange(128)[None, :]
    c["qmaj_mask"] = np.where(jj <= qq, 0.0, np.float32(NEG)).astype(np.float32)
    bf = np.zeros((3, 128, 640), dtype=np.float32)
    r_ = np.arange(128)[:, None]
    c_ = np.arange(640)[None, :]
    for oi, off in enumerate((0, 128, 256)):
        bf[oi] = np.where(np.abs(r_ + off - c_) <= 256, 0.0, np.float32(NEG))
    c["bandf_masks"] = np.ascontiguousarray(bf.transpose(1, 0, 2).reshape(128, 3 * 640)).astype(np.float32)
    c["i128"] = np.eye(128, dtype=np.float32)
    return c


def build_core(nc, n_sub=4, seqs=4, dump_x=False, debug_stage=None, chain=("attn_a", "ffn", "attn_b", "ffn"), paranoid=0):
    xin = nc.dram_tensor("x", [seqs, 128, DT * T], F32, kind="ExternalInput").ap()
    y = nc.dram_tensor("y", [seqs, 128, DT * T], F32, kind="ExternalOutput").ap()
    wqkv_a = nc.dram_tensor("wqkv_a", [D, 3 * D], F32, kind="ExternalInput").ap()
    bqkv_a = nc.dram_tensor("bqkv_a", [3 * D, 1], F32, kind="ExternalInput").ap()
    smallp_in = nc.dram_tensor("smallpack", [128, 32], F32, kind="ExternalInput").ap()
    wg8_in = nc.dram_tensor("wg8", [128, 8], F32, kind="ExternalInput").ap()
    wout_a = nc.dram_tensor("wout_a", [D, D], F32, kind="ExternalInput").ap()
    bout_a = nc.dram_tensor("bout_a", [D, 1], F32, kind="ExternalInput").ap()
    wg_a = nc.dram_tensor("wg_a", [D, 1], F32, kind="ExternalInput").ap()
    bg_a = nc.dram_tensor("bg_a", [1, 1], F32, kind="ExternalInput").ap()
    wqkv_b = nc.dram_tensor("wqkv_b", [D, 3 * D], F32, kind="ExternalInput").ap()
    wproj_b = nc.dram_tensor("wproj_b", [D, D], F32, kind="ExternalInput").ap()
    k0a = nc.dram_tensor("k0a", [D, F1], U8, kind="ExternalInput").ap()
    k0b = nc.dram_tensor("k0b", [D, F1], U8, kind="ExternalInput").ap()
    k1 = nc.dram_tensor("k1", [F1, F1], U8, kind="ExternalInput").ap()
    k2 = nc.dram_tensor("k2", [F1, D], U8, kind="ExternalInput").ap()
    s0in = nc.dram_tensor("s0", [1, 1], F32, kind="ExternalInput").ap()
    s1in = nc.dram_tensor("s1", [1, 1], F32, kind="ExternalInput").ap()
    s2in = nc.dram_tensor("s2", [1, 1], F32, kind="ExternalInput").ap()
    band_m = nc.dram_tensor("band_masks", [128, 8 * 512], U8, kind="ExternalInput").ap()
    bandf_m = nc.dram_tensor("bandf_masks", [128, 3 * 640], F32, kind="ExternalInput").ap()
    cst_m = nc.dram_tensor("cst_masks", [128, 4 * 512], U8, kind="ExternalInput").ap()
    qmaj_m = nc.dram_tensor("qmaj_mask", [128, 128], F32, kind="ExternalInput").ap()
    i128_in = nc.dram_tensor("i128", [128, 128], F32, kind="ExternalInput").ap()
    xbuf = nc.dram_tensor("xbuf", [seqs, 128, DT * T], F32, kind="Internal").ap()
    v8scr = nc.dram_tensor("v8scr", [1, 1024], F32, kind="Internal").ap()
    w0s = nc.dram_tensor("w0s", [DT, 128, F1], BF16, kind="Internal").ap()
    w1s = nc.dram_tensor("w1s", [8, FT, 128, 512], BF16, kind="Internal").ap()
    w2s = nc.dram_tensor("w2s", [FT, 128, D], BF16, kind="Internal").ap()
    xdump = nc.dram_tensor("xdump", [n_sub, seqs, 128, DT * T], F32,
                           kind="ExternalOutput").ap() if dump_x else None
    enddump = nc.dram_tensor("enddump", [8, 128, 1024], F32, kind="ExternalOutput").ap() if dump_x else None
    dbg = nc.dram_tensor("dbg", [4, 128, DT * T], F32, kind="ExternalOutput").ap() \
        if debug_stage else None

    p = P(nc)
    for sname in ("PE", "DV", "AC", "GP",
                  "CS", "CG", "CV",
                  "CA0", "CA1",
                  "WQ", "WO",
                  "W00", "W01", "W10", "W11", "W2",
                  "XG0", "XG1", "XC0", "XC1",
                  "XS0", "XS1", "XO",
                  "XF0", "XF1", "XF2", "XF3", "XF4", "XF5", "XF6", "XF7",
                  "MR", "NM", "V8", "GR", "XD"):
        p.sem(sname)

    arena = nc.alloc_sbuf_tensor("arena", [128, 212800], U8)
    base = nc.lookup_mloc(arena).addr
    _off = [0]

    def alloc(name, shape, dtype, align=32):
        _off[0] = (_off[0] + align - 1) // align * align
        h = nc.alloc_sbuf_tensor_at(name, list(shape), dtype, offset=base + _off[0])
        _off[0] += int(np.prod(shape[1:])) * mybir.dt.size(dtype)
        return h

    band = alloc("band", [128, 8 * 512], U8, align=512)
    cst = alloc("cst", [128, 4 * 512], U8, align=512)
    qmaj = alloc("qmaj", [128, 256], BF16, align=512)
    i128 = alloc("i128b", [128, 256], BF16, align=512)
    bandf = alloc("bandf", [128, 2048], BF16, align=512)
    smallp = alloc("smallp", [128, 128], F32, align=512)
    wg_bf = alloc("wg_bf", [128, 256], BF16, align=512)
    bq_s = smallp[:, 0:8]
    bk_s = smallp[:, 8:16]
    bo_pp = smallp[:, 16:24]
    srow = smallp[0:1, 24:32]
    onesc = alloc("onesc", [128, 1], F32, align=512)
    onescb = alloc("onescb", [128, 1], BF16)
    onesr = alloc("onesr", [128, 128], F32)
    bv_b = alloc("bv_b", [128, 1024], BF16)
    sc01 = alloc("sc01", [128, 1], F32)
    sc12 = alloc("sc12", [128, 1], F32)
    nbg = alloc("nbg", [1, 1], F32)
    c_eps = alloc("c_eps", [1, 1], F32)
    lnal = alloc("lnal", [1, 1024], F32)
    lnbe = alloc("lnbe", [1, 1024], F32)
    lnt = alloc("lnt", [128, 2 * 512], F32)
    lnm = lnt[0:1, 0:1024]
    m4 = alloc("m4", [128, 16], BF16)
    onesbr = alloc("onesbr", [128, 128], BF16)
    per = (_off[0] + 31) // 32 * 32

    def at(name, shape, dtype, off):
        esz = mybir.dt.size(dtype)
        boff = per + off
        assert boff % esz == 0, (name, boff, esz)
        cols = int(np.prod(shape[1:]))
        return arena[:, :].bitcast(dtype)[:, boff // esz: boff // esz + cols]

    qT = at("qT", [128, DT * T], BF16, 0)
    kT = at("kT", [128, DT * T], BF16, 32 * KB)
    vtok = at("vtok", [128, H * 16 * 65], BF16, 64 * KB)
    hT = at("hT", [128, DT * 1024], BF16, 97 * KB)
    wout_sb = at("wout_sb", [128, DT * 1024], BF16, 97 * KB)
    sq = at("sq", [128, 2 * 1024], BF16, 113 * KB)
    ebuf = at("ebuf", [128, 6 * 512], BF16, 117 * KB)
    epi = at("epi", [128, 2 * 512], F32, 117 * KB)
    wbuf = at("wbuf", [128, 8 * 512], BF16, 123 * KB)
    xc = at("xc", [128, 2 * 512], F32, 131 * KB)
    o_qg = at("o_qg", [128, DT * 512], BF16, 135 * KB)
    xg = at("xg", [128, 2 * 1024], F32, 135 * KB)
    grow = at("grow", [128, T], F32, 143 * KB)
    urow = at("urow", [128, T], F32, 151 * KB)
    wbuf2 = at("wbuf2", [128, 8 * 512], BF16, 151 * KB)
    v8 = at("v8", [128, 1024], F32, 159 * KB)
    mrow = at("mrow", [128, 1024], F32, 163 * KB + 128)
    wrow = at("wrow", [128, 1024], F32, 167 * KB + 128)
    erow = [at(f"erow{i}", [128, 512], F32, 171 * KB + 128 + 2048 * i) for i in range(3)]
    nmrow = at("nmrow", [128, 2048], BF16, 113 * KB)
    ngb = at("ngb", [128, 1024], F32, 113 * KB)
    mrow2 = at("mrow2", [128, 1024], F32, 143 * KB + 128)
    wrow2 = at("wrow2", [128, 1024], F32, 147 * KB + 128)
    v8c = at("v8c", [128, 16], F32, 171 * KB + 128 + 6144 + 384)
    nla = at("nla", [128, 1024], F32, 163 * KB + 128)
    nga = at("nga", [128, 1024], F32, 167 * KB + 128)
    stga = at("stga", [128, 128], F32, 143 * KB + 128)
    attn_end = 171 * KB + 128 + 2048 * 3 + 384 + 512
    a1T = at("a1T", [128, FT * 1024], BF16, 0)
    h0T = at("h0T", [128, DT * 1024], BF16, 64 * KB)
    outh = at("outh", [128, DT * 1024], F32, 80 * KB)
    a2s = at("a2s", [128, 2 * 2048], BF16, 112 * KB)
    sqf = at("sqf", [128, 2 * 1024], BF16, 112 * KB)
    w1b = at("w1b", [128, 2 * FT * 256], BF16, 120 * KB)
    w2b = at("w2b", [128, 4 * 1024], BF16, 152 * KB)
    w0b = at("w0b", [128, 2 * 8 * 512], BF16, 160 * KB)
    ffn_end = 176 * KB
    assert per + max(attn_end, ffn_end) <= 212800, (per, attn_end, ffn_end)

    pball = nc.alloc_psum_tensor("pball", [128, 4096], F32)
    pb = [pball[:, 512 * i:512 * (i + 1)] for i in range(8)]
    vre = vtok[:, :].rearrange("p (h c i) -> p h c i", h=H, c=16, i=65)

    nc._mm_count = [0]
    nc._phase_marks = []
    nc._mm_lines = []

    def phase(label):
        nc._phase_marks.append((nc._mm_count[0], label))

    def mm(out, lhsT, rhs, start, stop, waits=(), inc=None, tp=None):
        nc._mm_count[0] += 1
        nc._mm_lines.append((sys._getframe(1).f_lineno, out.shape[0], out.shape[1]))
        return p.emit("tensor",
                      lambda e, a=out, l=lhsT, r=rhs, s=start, _t=stop, t=tp:
                      e.matmul(a, l, r, start=s, stop=_t, tile_position=t),
                      waits=waits, inc=inc)

    st = {
        "wbufs": [Z, Z], "hT_pe": Z, "vtok": Z, "qkT": Z, "sq": [Z, Z], "xg": [Z, Z],
        "xc": [Z, Z], "ebuf": [Z, Z, Z, Z, Z, Z], "epi": Z, "o_qg_w": Z, "o_qg_r": Z,
        "grow_dv": Z, "grow_sy": Z, "grow_use": Z, "v8_sy": Z, "v8_use": Z,
        "mrow": [Z, Z], "wrow_stage": [Z, Z], "wrow64": [Z, Z], "ngb": Z, "nla": Z,
        "erow": Z, "erow_pe": Z, "nm_use": Z,
        "a2s": [Z, Z], "w1b": [Z, Z], "w0b": [Z, Z], "w2b": Z,
        "outh_sy": Z, "lnrows": Z, "lnt": [Z, Z], "m4": Z, "wout": Z, "conv": Z, "xw": [],
    }

    def _init(e):
        e.memset(smallp[:], 0.0)
        e.memset(qmaj[:], 0.0)
        e.memset(i128[:], 0.0)
        e.memset(bandf[:], 0.0)
        e.memset(wg_bf[:], 0.0)
        return e.memset(bv_b[:], 0.0)
    dvinit = p.emit("vector", _init, inc="DV")
    p.emit("sync", lambda e: e.dma_start(band[:], band_m), inc="CS", incn=16)
    p.emit("gpsimd", lambda e: e.dma_start(bandf[:, 0:1920], bandf_m), waits=[dvinit], inc="CG", incn=16)
    p.emit("sync", lambda e: e.dma_start(cst[:], cst_m), inc="CS", incn=16)
    p.emit("gpsimd", lambda e: e.dma_start(qmaj[:, 0:128], qmaj_m), inc="CG", incn=16)
    p.emit("gpsimd", lambda e: e.dma_start(i128[:, 0:128], i128_in), inc="CG", incn=16)
    p.emit("gpsimd", lambda e: e.dma_start(wg_bf[:, 0:8], wg8_in), inc="CG", incn=16)
    gq_bv = p.emit("gpsimd", lambda e: e.dma_start(bv_b[0:1, :], bqkv_a[2048:3072, :].rearrange("a b -> b a")), inc="CG", incn=16)
    sy0 = p.emit("sync", lambda e: e.dma_start(smallp[:, 0:32], smallp_in), waits=[dvinit], inc="CS", incn=16)
    p.emit("vector", lambda e: e.memset(onesc[:], 1.0))
    p.emit("vector", lambda e: e.memset(onescb[:], 1.0))
    p.emit("vector", lambda e: e.memset(onesr[:], 1.0))
    p.emit("vector", lambda e: e.memset(c_eps[:], 1.0e-5))
    p.emit("vector", lambda e: e.memset(onesbr[:], 1.0))

    def _sc(e):
        e.tensor_scalar_mul(srow[0:1, 4:5], srow[0:1, 0:1], 4.0)
        e.tensor_scalar(srow[0:1, 5:6], srow[0:1, 1:2], srow[0:1, 2:3], 2.0, ALU.mult, ALU.mult)
        e.tensor_scalar_mul(nbg[:], srow[0:1, 3:4], -1.0)
        return e.tensor_scalar_mul(bq_s[:], bq_s[:], 0.125)
    dv0 = p.emit("vector", _sc, waits=[sy0], inc="DV")
    mm(pb[0][:, 0:1], onesr[0:1, :], srow[0:1, 4:5], True, True, waits=[dv0])
    mm(pb[3][:, 0:1], onesr[0:1, :], srow[0:1, 5:6], True, True)
    dvob = p.emit("vector", lambda e: e.engine_nop(), waits=[gq_bv], inc="DV")
    mm(pb[1], onesbr[0:1, :], bv_b[0:1, 0:512], True, True, waits=[dvob])
    pe0 = mm(pb[2], onesbr[0:1, :], bv_b[0:1, 512:1024], True, True, inc="PE")

    def _cpb(e):
        e.tensor_copy(sc01[:], pb[0][:, 0:1])
        e.tensor_copy(sc12[:], pb[3][:, 0:1])
        e.tensor_copy(bv_b[:, 0:512], pb[1])
        return e.tensor_copy(bv_b[:, 512:1024], pb[2])
    dvc = p.emit("vector", _cpb, waits=[pe0], inc="DV")
    for bk in (0, 1, 2, 3):
        p.consumed(bk, dvc)

    ca = [at("ca0", [128, 4096], U8, 0), at("ca1", [128, 4096], U8, 4096)]
    cb = [at("cb0", [128, 4096], U8, 8192), at("cb1", [128, 4096], U8, 12288)]
    co = [at("co0", [128, 4096], BF16, 16384), at("co1", [128, 4096], BF16, 24576)]
    jobs = [("w0", k) for k in range(DT)] + [("w1", k) for k in range(FT)] + [("w2", k) for k in range(FT)]
    last_store = {0: Z, 1: Z}
    for ji, (kind, kk) in enumerate(jobs):
        par = ji % 2
        wld = [last_store[par]]
        if kind == "w0":
            p.emit("gpsimd", lambda e, kk=kk, par=par: e.dma_start(ca[par][:], k0a[128 * kk:128 * (kk + 1), :]),
                   waits=wld, inc="CA%d" % par, incn=16)
            lg = p.emit("gpsimd", lambda e, kk=kk, par=par: e.dma_start(cb[par][:], k0b[128 * kk:128 * (kk + 1), :]),
                        inc="CA%d" % par, incn=16)
            dv = p.emit("vector", lambda e, par=par: e.tensor_tensor(co[par][:], ca[par][:], cb[par][:], ALU.subtract),
                        waits=[lg], inc="DV")
            last_store[par] = p.emit("sync", lambda e, kk=kk, par=par: e.dma_start(w0s[kk], co[par][:]),
                                     waits=[dv], inc="CV", incn=16)
        else:
            src, ncol = (k1, F1) if kind == "w1" else (k2, D)
            lg = p.emit("gpsimd", lambda e, kk=kk, par=par, src=src, ncol=ncol:
                        e.dma_start(ca[par][:, 0:ncol], src[128 * kk:128 * (kk + 1), :]),
                        waits=wld, inc="CA%d" % par, incn=16)
            dv = p.emit("vector", lambda e, par=par, ncol=ncol:
                        e.tensor_scalar(co[par][:, 0:ncol], ca[par][:, 0:ncol], 2.0, -1.0, ALU.mult, ALU.add),
                        waits=[lg], inc="DV")
            if kind == "w1":
                for s2 in range(8):
                    last_store[par] = p.emit("sync", lambda e, kk=kk, par=par, s2=s2:
                                             e.dma_start(w1s[s2, kk], co[par][:, 512 * s2:512 * (s2 + 1)]),
                                             waits=[dv] if s2 == 0 else [], inc="CV", incn=16)
            else:
                last_store[par] = p.emit("sync", lambda e, kk=kk, par=par:
                                         e.dma_start(w2s[kk], co[par][:, 0:D]),
                                         waits=[dv], inc="CV", incn=16)
    st["conv"] = p.v("CV")
    _marks = [p.v("CV"), p.v("CS"), p.v("CG")]
    p.wait("tensor", _marks)
    p.wait("vector", _marks)
    p.wait("scalar", _marks)

    DMAS = ("CS", "CG", "CV", "CA0", "CA1", "WQ", "WO", "W00", "W01", "W10", "W11", "W2",
            "XG0", "XG1", "XC0", "XC1", "XS0", "XS1", "XO",
            "XF0", "XF1", "XF2", "XF3", "XF4", "XF5", "XF6", "XF7", "MR", "NM", "V8", "GR")

    def joinall():
        marks = [p.v(x) for x in ("PE", "DV", "AC") + DMAS]
        for eng in ("tensor", "vector", "scalar", "sync", "gpsimd"):
            p.wait(eng, marks)

    def emit_ln(seq, half, xsrc, ffn_mode):
        phase(f"LN{'f' if ffn_mode else 'a'}")
        hb = half * 1024
        sqb = sqf if ffn_mode else sq
        dst = h0T if ffn_mode else hT
        xg_pe = list(st["xg"])
        for k in range(DT):
            par = k % 2
            if ffn_mode:
                xv = outh[:, 1024 * k:1024 * (k + 1)]
                lw = []
            else:
                xv = xg[:, par * 1024:(par + 1) * 1024]
                lw = [xg_pe[par], st["o_qg_r"]]
            lx = p.emit("sync", lambda e, xv=xv, k=k: e.dma_start(xv, xsrc[seq, :, T * k + hb:T * k + hb + 1024]),
                        waits=lw + st["xw"], inc=("XF%d" % k) if ffn_mode else ("XG%d" % par), incn=16)
            sv = sqb[:, par * 1024:(par + 1) * 1024]
            sqw = [st["sq"][par]]
            if k < 2:
                sqw += [st["a2s"][0], st["a2s"][1]]
            ac = p.emit("scalar", lambda e, sv=sv, xv=xv: e.activation(sv, xv, AF.Square),
                        waits=[lx] + sqw, inc="AC")
            last = None
            for c in range(2):
                mm(pb[4 + c][0:1, :], onesc[:, 0:1],
                   xv[:, 512 * c:512 * (c + 1)], k == 0, k == DT - 1,
                   waits=(p.fillw(4 + c) + [lx]) if k == 0 else [])
                last = mm(pb[6 + c][0:1, :], onescb[:, 0:1], sv[:, 512 * c:512 * (c + 1)],
                          k == 0, k == DT - 1,
                          waits=(p.fillw(6 + c) + [ac]) if k == 0 else ([ac] if c == 0 else []),
                          inc="PE" if c == 1 else None)
            xg_pe[par] = last
            st["sq"][par] = last
        if not ffn_mode:
            st["xg"] = xg_pe
        pe_stats = p.v("PE")

        def _rows(e):
            e.tensor_scalar_mul(lnm[0:1, 0:512], pb[4][0:1, :], 1.0 / D)
            e.tensor_scalar_mul(lnm[0:1, 512:1024], pb[5][0:1, :], 1.0 / D)
            e.tensor_scalar_mul(lnal[0:1, 0:512], pb[6][0:1, :], 1.0 / D)
            e.tensor_scalar_mul(lnal[0:1, 512:1024], pb[7][0:1, :], 1.0 / D)
            e.tensor_tensor(lnbe[0:1, :], lnm[0:1, :], lnm[0:1, :], ALU.mult)
            return e.tensor_tensor(lnal[0:1, :], lnal[0:1, :], lnbe[0:1, :], ALU.subtract)
        dvr = p.emit("vector", _rows, waits=[pe_stats, st["lnrows"], st["lnt"][0], st["lnt"][1]], inc="DV")
        for bk in (4, 5, 6, 7):
            p.consumed(bk, dvr)
        acs = p.emit("scalar", lambda e: e.activation(lnbe[0:1, :], lnal[0:1, :], AF.Sqrt, bias=c_eps[0:1, 0:1]),
                     waits=[dvr], inc="AC")

        def _rows2(e):
            e.reciprocal(lnal[0:1, :], lnbe[0:1, :])
            return e.scalar_tensor_tensor(lnbe[0:1, :], lnm[0:1, :], -1.0, lnal[0:1, :], ALU.mult, ALU.mult)
        dvr2 = p.emit("vector", _rows2, waits=[acs], inc="DV")
        st["lnrows"] = dvr2
        dv_last = Z
        first_h = True
        for c in range(2):
            ba, bb_ = 4 + 2 * c, 5 + 2 * c
            mm(pb[ba], onesr[0:1, :], lnal[0:1, 512 * c:512 * (c + 1)],
               True, True, waits=p.fillw(ba) + [dvr2])
            peb = mm(pb[bb_], onesr[0:1, :], lnbe[0:1, 512 * c:512 * (c + 1)],
                     True, True, waits=p.fillw(bb_), inc="PE")
            for k in range(DT):
                par = k % 2
                if ffn_mode:
                    xv = outh[:, 1024 * k + 512 * c:1024 * k + 512 * (c + 1)]
                    lxw = []
                else:
                    xv = xc[:, 512 * par:512 * (par + 1)]
                    lx = p.emit("sync", lambda e, xv=xv, k=k, c=c:
                                e.dma_start(xv, xsrc[seq, :, T * k + hb + 512 * c:T * k + hb + 512 * (c + 1)]),
                                waits=[st["xc"][par]] + st["xw"], inc="XC%d" % par, incn=16)
                    lxw = [lx]
                lt = lnt[:, 512 * par:512 * par + 512]
                w1_ = lxw + ([peb] if k == 0 else []) + [st["lnt"][par]]
                p.emit("vector", lambda e, xv=xv, lt=lt, ba=ba: e.tensor_tensor(lt, xv, pb[ba], ALU.mult),
                       waits=w1_)
                hw = []
                if first_h:
                    hw = [st["outh_sy"], st["hT_pe"]]
                    first_h = False
                dv_last = p.emit("vector", lambda e, lt=lt, k=k, c=c, bb_=bb_:
                                 e.tensor_tensor(dst[:, 1024 * k + 512 * c:1024 * k + 512 * (c + 1)], lt, pb[bb_], ALU.add),
                                 waits=hw, inc="DV")
                st["lnt"][par] = dv_last
                if not ffn_mode:
                    st["xc"][par] = dv_last
            p.consumed(ba, dv_last)
            p.consumed(bb_, dv_last)
            st["lnrows"] = peb
            if c == 0:
                dv_c0 = dv_last
        return dv_c0, dv_last

    def emit_qkv(seq, half, is_a, hdv0, hdv1):
        phase("QKV")
        hb = half * 1024
        wsrc = wqkv_a if is_a else wqkv_b
        bufs = (wbuf, wbuf2)
        st["wbufs"] = list(st["wbufs"])
        loaded = [None] * 6

        def qload(scc):
            par = scc % 2
            buf = bufs[par]
            c0 = 512 * scc
            marks = []
            for kk in range(DT):
                marks.append(p.emit("gpsimd", lambda e, kk=kk, c0=c0, buf=buf:
                             e.dma_start(buf[:, 512 * kk:512 * (kk + 1)],
                                         wsrc[128 * kk:128 * (kk + 1), c0:c0 + 512]),
                             waits=[st["wbufs"][par]] if kk == 0 else [], inc="WQ", incn=16))
            loaded[scc] = marks
        qload(0)
        qload(1)
        for sc in range(4):
            par = sc % 2
            buf = bufs[par]
            marks = loaded[sc]
            for tc in range(2):
                for dp in range(2):
                    for dd in range(2):
                        dt_sc = 2 * dp + dd
                        bk = 4 * dp + 2 * dd + tc
                        pe_f = Z
                        for kk in range(DT):
                            pe_f = mm(pb[bk],
                                      buf[:, 512 * kk + 128 * dt_sc:512 * kk + 128 * (dt_sc + 1)],
                                      hT[:, 1024 * kk + 512 * tc:1024 * kk + 512 * (tc + 1)],
                                      kk == 0, kk == DT - 1,
                                      waits=(p.fillw(bk) + [marks[kk], hdv0 if tc == 0 else hdv1])
                                      if kk == 0 else [marks[kk]],
                                      inc="PE" if kk == DT - 1 else None)
                        dt_glob = 4 * sc + dt_sc
                        dst = qT if dt_glob < 8 else kT
                        dt_l = dt_glob % 8
                        col = T * dt_l + hb + 512 * tc
                        scale = 0.125 if dt_glob < 8 else 1.0
                        if is_a:
                            bias = (bq_s if dt_glob < 8 else bk_s)[:, dt_l:dt_l + 1]
                            ac = p.emit("vector", lambda e, bk=bk, dst=dst, col=col, bias=bias, scale=scale:
                                        e.tensor_scalar(dst[:, col:col + 512], pb[bk], scale, bias,
                                                        ALU.mult, ALU.add),
                                        waits=[pe_f], inc="DV")
                        else:
                            ac = p.emit("scalar", lambda e, bk=bk, dst=dst, col=col, scale=scale:
                                        e.activation(dst[:, col:col + 512], pb[bk], AF.Copy, scale=scale),
                                        waits=[pe_f], inc="AC")
                        p.consumed(bk, ac)
            st["wbufs"][par] = p.v("PE")
            if sc + 2 < 6:
                qload(sc + 2)
        for sc in range(2):
            par = sc % 2
            buf = bufs[par]
            marks = loaded[4 + sc]
            for tt in range(8):
                ttg = 8 * half + tt
                bk = tt
                pe_f = Z
                for kk in range(DT):
                    pe_f = mm(pb[bk],
                              hT[:, 1024 * kk + 128 * tt:1024 * kk + 128 * (tt + 1)],
                              buf[:, 512 * kk:512 * (kk + 1)],
                              kk == 0, kk == DT - 1,
                              waits=(p.fillw(bk) + [marks[kk], hdv0 if tt < 4 else hdv1])
                              if kk == 0 else [marks[kk]],
                              inc="PE" if kk == DT - 1 else None)
                dstv = vre[:, 8 * sc:8 * (sc + 1), ttg, 0:64]
                srcv = pb[bk].rearrange("p (h i) -> p h i", h=8, i=64)

                def _ev(e, dstv=dstv, srcv=srcv, sc=sc, is_a=is_a):
                    if is_a:
                        bvv = bv_b[:, 512 * sc:512 * (sc + 1)].rearrange("p (h i) -> p h i", h=8, i=64)
                        return e.tensor_tensor(dstv, srcv, bvv, ALU.add)
                    return e.tensor_copy(dstv, srcv)
                dv = p.emit("vector", _ev, waits=[pe_f, st["outh_sy"]], inc="DV")
                p.consumed(bk, dv)
            st["wbufs"][par] = p.v("PE")
        if half == 1:
            st["vtok"] = p.emit("vector", lambda e: e.memset(vre[:, :, :, 64:65], 1.0), inc="DV")
        return

    def emit_gate(seq, hT_dv, half):
        hb = half * 1024
        for c in range(2):
            bk = 4 + c
            pe_f = Z
            for kk in range(DT):
                pe_f = mm(pb[bk][0:1, :], wg_bf[:, kk:kk + 1],
                          hT[:, 1024 * kk + 512 * c:1024 * kk + 512 * (c + 1)],
                          kk == 0, kk == DT - 1,
                          waits=(p.fillw(bk) + [hT_dv]) if kk == 0 else [],
                          inc="PE" if kk == DT - 1 else None)
            ac = p.emit("scalar", lambda e, c=c, bk=bk, hb=hb:
                        e.activation(grow[0:1, hb + 512 * c:hb + 512 * (c + 1)], pb[bk][0:1, :],
                                     AF.Exp, bias=nbg[0:1, 0:1], scale=-1.0),
                        waits=[pe_f, st["grow_dv"], st["grow_sy"]], inc="AC")
            p.consumed(bk, ac)
        if half == 1:
            acg = p.v("AC")

            def _g(e):
                e.tensor_scalar_add(urow[0:1, :], grow[0:1, :], 1.0)
                e.reciprocal(urow[0:1, :], urow[0:1, :])
                return e.tensor_tensor(grow[0:1, :], grow[0:1, :], urow[0:1, :], ALU.mult)
            dvg = p.emit("vector", _g, waits=[acg], inc="DV")
            st["grow_dv"] = dvg
            p.emit("sync", lambda e: e.dma_start(grow[64:65, :], urow[0:1, :]),
                   waits=[dvg, st["grow_use"]], inc="GR", incn=16)
            gsy = p.emit("sync", lambda e: e.dma_start(urow[64:65, :], grow[0:1, :]),
                         inc="GR", incn=16)
            st["grow_sy"] = gsy
            return gsy
        return Z

    def emit_v8(seq):
        pe_f = Z
        for c in range(16):
            for hset in range(2):
                rhs = vre[:, 8 * hset:8 * (hset + 1), c, 0:64]
                pe_f = mm(pb[6 + hset][0:1, :], onescb[:, 0:1], rhs, c == 0, c == 15,
                          waits=(p.fillw(6 + hset) + [st["vtok"]]) if c == 0 else [],
                          inc="PE" if (c == 15 and hset == 1) else None)

        def _v8(e):
            e.tensor_scalar_mul(v8[0:1, 0:512], pb[6][0:1, :], 1.0e-8)
            return e.tensor_scalar_mul(v8[0:1, 512:1024], pb[7][0:1, :], 1.0e-8)
        dv = p.emit("vector", _v8, waits=[pe_f, st["v8_sy"]], inc="DV")
        p.consumed(6, dv)
        p.consumed(7, dv)
        sys_ = p.emit("sync", lambda e: e.dma_start(v8scr, v8[0:1, 0:1024]),
                      waits=[dv, st["v8_use"]], inc="V8", incn=16)
        syc = sys_
        for h in range(16):
            syc = p.emit("sync", lambda e, h=h: e.dma_start(
                         v8c[0:64, h:h + 1],
                         v8scr[:, 64 * h:64 * h + 64].rearrange("p (d one) -> (p d) one", one=1)),
                         waits=[syc], inc="V8", incn=16)
        st["v8_sy"] = syc
        return syc

    def emit_proj(seq, qg, is_a, o_dvs, xsrc, xdst, sub):
        phase("PROJ")
        q0 = 512 * qg
        lxs = {}

        def xload(dt):
            par = dt % 2
            lxs[dt] = p.emit("sync", lambda e, dt=dt, par=par, xsrc=xsrc:
                             e.dma_start(xc[:, 512 * par:512 * (par + 1)],
                                         xsrc[seq, :, T * dt + q0:T * dt + q0 + 512]),
                             waits=[st["xc"][par]] + st["xw"], inc="XC%d" % par, incn=16)
        xload(0)
        xload(1)
        for dt in range(DT):
            bk = dt % 4
            pe_f = Z
            for kk in range(DT):
                pe_f = mm(pb[bk], wout_sb[:, 1024 * kk + 128 * dt:1024 * kk + 128 * (dt + 1)],
                          o_qg[:, 512 * kk:512 * (kk + 1)],
                          kk == 0, kk == DT - 1,
                          waits=(p.fillw(bk) + [o_dvs[kk], st["wout"]]) if kk == 0 else [o_dvs[kk]],
                          inc="PE" if kk == DT - 1 else None)
            par = dt % 2
            lx = lxs[dt]
            if is_a:
                dv = p.emit("vector", lambda e, bk=bk, dt=dt, par=par:
                            e.scalar_tensor_tensor(xc[:, 512 * par:512 * (par + 1)], pb[bk],
                                                   bo_pp[:, dt:dt + 1], xc[:, 512 * par:512 * (par + 1)],
                                                   ALU.add, ALU.add),
                            waits=[pe_f, lx], inc="DV")
            else:
                dv = p.emit("vector", lambda e, bk=bk, par=par:
                            e.tensor_tensor(xc[:, 512 * par:512 * (par + 1)], pb[bk],
                                            xc[:, 512 * par:512 * (par + 1)], ALU.add),
                            waits=[pe_f, lx], inc="DV")
            p.consumed(bk, dv)
            sy = p.emit("sync", lambda e, dt=dt, par=par, xdst=xdst:
                        e.dma_start(xdst[seq, :, T * dt + q0:T * dt + q0 + 512],
                                    xc[:, 512 * par:512 * (par + 1)]),
                        waits=[dv], inc="XS%d" % par, incn=16)
            if xdump is not None:
                sy = p.emit("sync", lambda e, dt=dt, par=par, sub=sub:
                            e.dma_start(xdump[sub, seq, :, T * dt + q0:T * dt + q0 + 512],
                                        xc[:, 512 * par:512 * (par + 1)]),
                            inc="XS%d" % par, incn=16)
            st["xc"][par] = sy
            if dt + 2 < DT:
                xload(dt + 2)
        st["o_qg_r"] = p.v("PE")

    def emit_attn_a(seq, gsy, xsrc, xdst, sub):
        vdv = st["vtok"]
        def maxpass(g, qg):
                phase("AMAX")
                dv_m = Z
                for qt in range(4):
                    qi = 4 * qg + qt
                    w0 = max(0, 128 * qi - 256)
                    w1 = min(T, 128 * qi + 384)
                    wlen = w1 - w0
                    off_id = {0: 0, 128: 1, 256: 2}[128 * qi - w0]
                    rot = (qt % 2) * 2048
                    for hh in range(2):
                        pbase = 1024 * hh + rot
                        qsl = qT[64 * hh:64 * (hh + 1), T * g + 128 * qi:T * g + 128 * (qi + 1)]
                        pe_q = Z
                        c0 = 0
                        while c0 < wlen:
                            cw = min(512, wlen - c0)
                            pe_q = mm(pball[:, pbase + c0:pbase + c0 + cw], qsl,
                                      kT[64 * hh:64 * (hh + 1), T * g + w0 + c0:T * g + w0 + c0 + cw],
                                      True, True,
                                      waits=p.fillw(pbase // 512) + (p.fillw(pbase // 512 + 1) if c0 else []) + [st["qkT"]],
                                      tp=(64 * hh, 0), inc="PE")
                            c0 += cw
                        pe_q = mm(pball[:, pbase + 640:pbase + 704], qsl,
                                  kT[64 * hh:64 * (hh + 1), T * g:T * g + 64],
                                  True, True,
                                  waits=p.fillw(pbase // 512 + 1) if wlen <= 512 else [],
                                  tp=(64 * hh, 0), inc="PE")
                        if off_id == 2 and wlen == 640:
                            p.emit("vector", lambda e, pbase=pbase:
                                   e.tensor_tensor(pball[:, pbase:pbase + 128], pball[:, pbase:pbase + 128],
                                                   bandf[:, 1280:1408], ALU.add),
                                   waits=[pe_q], inc="DV")
                            dvd = p.emit("vector", lambda e, pbase=pbase:
                                         e.tensor_tensor(pball[:, pbase + 512:pbase + 640], pball[:, pbase + 512:pbase + 640],
                                                         bandf[:, 1792:1920], ALU.add),
                                         inc="DV")
                        else:
                            dvd = p.emit("vector", lambda e, pbase=pbase, wlen=wlen, off_id=off_id:
                                         e.tensor_tensor(pball[:, pbase:pbase + wlen], pball[:, pbase:pbase + wlen],
                                                         bandf[:, 640 * off_id:640 * off_id + wlen], ALU.add),
                                         waits=[pe_q], inc="DV")
                        dv_m = p.emit("vector", lambda e, pbase=pbase, wlen=wlen, hh=hh, qt=qt:
                                      e.reduce_max(m4[:, 4 * hh + qt:4 * hh + qt + 1],
                                                   pball[:, pbase:pbase + wlen], axis=AX.X),
                                      waits=[dvd, st["m4"]], inc="DV")
                        dv_m = p.emit("vector", lambda e, pbase=pbase, hh=hh, qt=qt:
                                      e.reduce_max(m4[:, 8 + 4 * hh + qt:8 + 4 * hh + qt + 1],
                                                   pball[:, pbase + 640:pbase + 704], axis=AX.X),
                                      inc="DV")
                        p.consumed(pbase // 512, dv_m)
                        p.consumed(pbase // 512 + 1, dv_m)
                pe_t = mm(pb[7][0:16, 0:128], m4[:, 0:16], i128[:, 0:128], True, True,
                          waits=p.fillw(7) + [dv_m], inc="PE")
                st["m4"] = pe_t
                dvt = p.emit("vector", lambda e: e.tensor_scalar_mul(stga[32:48, 0:128], pb[7][0:16, 0:128], -1.0),
                             waits=[pe_t, st["wrow_stage"][0], st["wrow_stage"][1]], inc="DV")
                p.consumed(7, dvt)
                sy2 = p.emit("gpsimd", lambda e: e.dma_start(nmrow[64:65, 0:2048], stga[32:48, 0:128]),
                             waits=[dvt, st["nm_use"]], inc="NM", incn=16)
                st["wrow_stage"][0] = sy2
                return sy2

        nms0 = maxpass(0, 0)
        for qg in range(4):
            q0 = 512 * qg
            nms = [None] * 8
            nms[0] = nms0
            o_dvs = [Z] * 8
            for g in range(8):
                sy2 = nms[g]
                phase("AGLB")
                pe_g = Z
                for hh in range(2):
                    mm(pb[hh][0:64, :], kT[64 * hh:64 * (hh + 1), T * g:T * g + 64],
                       qT[64 * hh:64 * (hh + 1), T * g + q0:T * g + q0 + 512],
                       True, False, waits=p.fillw(hh) + [st["qkT"]],
                       tp=(64 * hh, 0))
                    pe_g = mm(pb[hh][0:64, :], onesbr[64:65, 0:64],
                              nmrow[64:65, 1024 + 512 * hh:1024 + 512 * (hh + 1)],
                              False, True, waits=[sy2], tp=(64, 0), inc="PE" if hh == 1 else None)
                ac_g = []
                for hh in range(2):
                    sl = 2 + hh
                    ac = p.emit("scalar", lambda e, hh=hh, sl=sl:
                                e.activation(ebuf[0:64, 512 * sl:512 * (sl + 1)], pb[hh][0:64, :], AF.Exp),
                                waits=[pe_g, st["ebuf"][sl]], inc="AC")
                    ac_g.append(ac)
                    p.consumed(hh, ac)
                pe_pvg = Z
                for hh in range(2):
                    h = 2 * g + hh
                    pe_pvg = mm(pb[6 + hh][0:65, :], vtok[0:64, (h * 16) * 65:(h * 16) * 65 + 65],
                                ebuf[0:64, 512 * (2 + hh):512 * (2 + hh) + 512],
                                True, True,
                                waits=p.fillw(6 + hh) + [ac_g[hh], vdv],
                                inc="PE" if hh == 1 else None)
                st["ebuf"][2] = pe_pvg
                st["ebuf"][3] = pe_pvg
                phase("ALOC")
                kcs = list(range(max(0, 4 * qg - 2), min(16, 4 * qg + 6)))
                pv_last = Z
                pq = []
                for i, kc in enumerate(kcs):
                    sA = 2 * (i % 3)
                    bA = 2 * (i % 2)
                    pe_st = Z
                    for hh in range(2):
                        bk = bA + hh
                        mm(pb[bk], kT[64 * hh:64 * (hh + 1), T * g + 128 * kc:T * g + 128 * (kc + 1)],
                           qT[64 * hh:64 * (hh + 1), T * g + q0:T * g + q0 + 512],
                           True, False, waits=p.fillw(bk), tp=(64 * hh, 0))
                        pe_st = mm(pb[bk], onesbr[64:65, 0:128],
                                   nmrow[64:65, 512 * hh:512 * (hh + 1)],
                                   False, True, waits=[sy2] if i == 0 else [], tp=(64, 0),
                                   inc="PE" if hh == 1 else None)
                    r = kc - 4 * qg + 2
                    dvms = []
                    for hh in range(2):
                        bk = bA + hh
                        sl = sA + hh
                        ac = p.emit("scalar", lambda e, bk=bk, sl=sl:
                                    e.activation(ebuf[:, 512 * sl:512 * (sl + 1)], pb[bk], AF.Exp),
                                    waits=[pe_st, st["ebuf"][sl]], inc="AC")
                        p.consumed(bk, ac)
                        dvm = p.emit("gpsimd", lambda e, sl=sl, r=r:
                                     e.tensor_tensor(ebuf[:, 512 * sl:512 * (sl + 1)],
                                                     ebuf[:, 512 * sl:512 * (sl + 1)],
                                                     band[:, 512 * r:512 * (r + 1)], ALU.mult),
                                     waits=[ac], inc="GP")
                        dvms.append(dvm)
                    def flush(last, kcs=kcs):
                        nonlocal pv_last
                        (pkc, psA, pdvms) = pq.pop(0)
                        for hh in range(2):
                            h = 2 * g + hh
                            sl = psA + hh
                            pv_last = mm(pb[4 + hh][0:65, :],
                                         vtok[:, (h * 16 + pkc) * 65:(h * 16 + pkc) * 65 + 65],
                                         ebuf[:, 512 * sl:512 * (sl + 1)],
                                         pkc == kcs[0], last,
                                         waits=[pdvms[hh]] + ((p.fillw(4 + hh) + [vdv]) if pkc == kcs[0] else []),
                                         inc="PE")
                            st["ebuf"][sl] = pv_last
                    pq.append((kc, sA, dvms))
                    if len(pq) > 2:
                        flush(False)
                while pq:
                    flush(len(pq) == 1)
                pe_pv = pv_last
                st["nm_use"] = pe_pv
                evl = []
                evg = []
                for hh in range(2):
                    ev = p.emit("scalar", lambda e, hh=hh:
                                e.activation(nla[0:65, 512 * hh:512 * (hh + 1)], pb[4 + hh][0:65, :], AF.Copy),
                                waits=[pe_pv, st["nla"]], inc="AC")
                    p.consumed(4 + hh, ev)
                    evl.append(ev)
                    ev2 = p.emit("scalar", lambda e, hh=hh:
                                 e.activation(nga[0:65, 512 * hh:512 * (hh + 1)], pb[6 + hh][0:65, :], AF.Copy),
                                 inc="AC")
                    p.consumed(6 + hh, ev2)
                    evg.append(ev2)
                if g + 1 < 8:
                    nms[g + 1] = maxpass(g + 1, qg)
                phase("BEPI")
                for hh in range(2):
                    def _rows1(e, hh=hh, q0=q0):
                        e.reciprocal(erow[0][64:65, :], nla[64:65, 512 * hh:512 * hh + 512])
                        e.reciprocal(erow[1][64:65, :], nga[64:65, 512 * hh:512 * hh + 512])
                        return e.tensor_tensor(erow[2][64:65, :], urow[64:65, q0:q0 + 512], erow[0][64:65, :], ALU.mult)
                    dvr1 = p.emit("vector", _rows1,
                                  waits=[evl[hh], evg[hh], gsy, st["erow"], st["erow_pe"]], inc="DV")
                    dvr = p.emit("vector", lambda e, q0=q0:
                                 e.tensor_tensor(erow[0][64:65, :], grow[64:65, q0:q0 + 512],
                                                 erow[1][64:65, :], ALU.mult), inc="DV")
                    st["erow"] = dvr
                    mm(pb[4][0:64, :], onesr[64:65, 0:64],
                       erow[2][64:65, :], True, True,
                       waits=p.fillw(4) + [dvr1], tp=(64, 0))
                    peb = mm(pb[5][0:64, :], onesr[64:65, 0:64],
                             erow[0][64:65, :], True, True,
                             waits=p.fillw(5) + [dvr], tp=(64, 0), inc="PE")
                    st["erow_pe"] = peb

                    def _blk(e, hh=hh, g=g):
                        e.tensor_tensor(epi[0:64, 0:512], nla[0:64, 512 * hh:512 * hh + 512],
                                        pb[4][0:64, :], ALU.mult)
                        e.tensor_tensor(epi[0:64, 512:1024], nga[0:64, 512 * hh:512 * hh + 512],
                                        pb[5][0:64, :], ALU.mult)
                        return e.tensor_tensor(o_qg[64 * hh:64 * (hh + 1), 512 * g:512 * (g + 1)],
                                               epi[0:64, 0:512], epi[0:64, 512:1024], ALU.add)
                    dvb = p.emit("vector", _blk, waits=[peb, st["epi"], st["o_qg_w"]], inc="DV")
                    st["epi"] = dvb
                    for sl in range(4):
                        st["ebuf"][sl] = dvb
                    for bk in (4, 5):
                        p.consumed(bk, dvb)
                    st["nla"] = dvb
                    o_dvs[g] = dvb
            if qg + 1 < 4:
                nms0 = maxpass(0, qg + 1)
            emit_proj(seq, qg, True, o_dvs, xsrc, xdst, sub)
            st["o_qg_w"] = st["o_qg_r"]
        st["grow_use"] = st["erow"]

    C1 = 1.0 + T * 1.0e-8
    C2 = (T * 1.0e-8) * 1.0e-6

    def emit_attn_b(seq, v8sy, xsrc, xdst, sub):
        vdv = st["vtok"]
        mrows = (mrow, mrow2)
        wrows = (wrow, wrow2)
        def maxpass(g, qg):
                par = g % 2
                phase("BMAX")
                dv_m = Z
                for qt in range(4):
                    qi = 4 * qg + qt
                    L = 128 * (qi + 1)
                    nch = (L + 511) // 512
                    rot = (qt % 2) * 2 if nch <= 2 else 0
                    pe_q = Z
                    for hh in range(2):
                        for kc5 in range(nch):
                            w = min(512, L - 512 * kc5)
                            bk = 4 * hh + rot + kc5
                            pe_q = mm(pb[bk][:, 0:w],
                                      qT[64 * hh:64 * (hh + 1), T * g + 128 * qi:T * g + 128 * (qi + 1)],
                                      kT[64 * hh:64 * (hh + 1), T * g + 512 * kc5:T * g + 512 * kc5 + w],
                                      True, True, waits=p.fillw(bk) + [st["qkT"]],
                                      tp=(64 * hh, 0),
                                      inc="PE" if (hh == 1 and kc5 == nch - 1) else None)
                    for hh in range(2):
                        lastb = 4 * hh + rot + nch - 1
                        wq = min(512, L - 512 * (nch - 1))
                        dvd = p.emit("vector", lambda e, lastb=lastb, wq=wq:
                                     e.tensor_tensor(pb[lastb][:, wq - 128:wq], pb[lastb][:, wq - 128:wq],
                                                     qmaj[:, 0:128], ALU.add),
                                     waits=[pe_q], inc="DV")
                        dv_m = p.emit("vector", lambda e, hh=hh, qt=qt, L=L, rot=rot:
                                      e.reduce_max(m4[:, 4 * hh + qt:4 * hh + qt + 1],
                                                   pball[:, 2048 * hh + 512 * rot:2048 * hh + 512 * rot + L], axis=AX.X),
                                      waits=[dvd, st["m4"]], inc="DV")
                        for kc5 in range(nch):
                            p.consumed(4 * hh + rot + kc5, dv_m)
                pe_t = mm(pb[7][0:8, 0:128], m4[:, 0:8], i128[:, 0:128],
                          True, True, waits=p.fillw(7) + [dv_m], inc="PE")
                st["m4"] = pe_t
                wst = wrows[par]
                dvt = p.emit("vector", lambda e, wst=wst: e.tensor_copy(wst[0:8, 0:128], pb[7][0:8, 0:128]),
                             waits=[pe_t, st["wrow_stage"][par]], inc="DV")
                p.consumed(7, dvt)
                sy2 = p.emit("sync", lambda e, par=par, wst=wst:
                             e.dma_start(mrows[par][64:65, 0:1024], wst[0:8, 0:128]),
                             waits=[dvt, st["mrow"][par]], inc="MR", incn=16)
                st["wrow_stage"][par] = sy2
                acw = p.emit("scalar", lambda e, par=par:
                             e.activation(wrows[par][64:65, 0:1024], mrows[par][64:65, 0:1024],
                                          AF.Exp, scale=-1.0),
                             waits=[sy2, st["wrow64"][par]], inc="AC")
                st["mrow"][par] = acw
                return acw

        acw0 = maxpass(0, 0)
        for qg in range(4):
            q0 = 512 * qg
            acws = [None] * 8
            acws[0] = acw0
            o_dvs = [Z] * 8
            for g in range(8):
                par = g % 2
                phase("BSTPV")
                nkc = 4 * (qg + 1)
                pv_last = Z
                pq = []
                for kc in range(nkc):
                    sA = 2 * (kc % 3)
                    bA = 2 * (kc % 2)
                    pe_st = Z
                    for hh in range(2):
                        bk = bA + hh
                        pe_st = mm(pb[bk], kT[64 * hh:64 * (hh + 1), T * g + 128 * kc:T * g + 128 * (kc + 1)],
                                   qT[64 * hh:64 * (hh + 1), T * g + q0:T * g + q0 + 512],
                                   True, True, waits=p.fillw(bk), tp=(64 * hh, 0),
                                   inc="PE" if hh == 1 else None)
                    diag = kc - 4 * qg
                    wms = []
                    for hh in range(2):
                        bk = bA + hh
                        sl = sA + hh
                        ac = p.emit("scalar", lambda e, bk=bk, sl=sl:
                                    e.activation(ebuf[:, 512 * sl:512 * (sl + 1)], pb[bk], AF.Exp),
                                    waits=[pe_st, st["ebuf"][sl]], inc="AC")
                        p.consumed(bk, ac)
                        wm = ac
                        if diag >= 0:
                            wm = p.emit("gpsimd", lambda e, sl=sl, diag=diag:
                                        e.tensor_tensor(ebuf[:, 512 * sl:512 * (sl + 1)],
                                                        ebuf[:, 512 * sl:512 * (sl + 1)],
                                                        cst[:, 512 * diag:512 * (diag + 1)], ALU.mult),
                                        waits=[ac], inc="GP")
                        wms.append(wm)
                    def flushb(last, nkc=nkc):
                        nonlocal pv_last
                        (pkc, psA, pwms) = pq.pop(0)
                        for hh in range(2):
                            h = 2 * g + hh
                            sl = psA + hh
                            pv_last = mm(pb[4 + hh][0:65, :],
                                         vtok[:, (h * 16 + pkc) * 65:(h * 16 + pkc) * 65 + 65],
                                         ebuf[:, 512 * sl:512 * (sl + 1)],
                                         pkc == 0, last,
                                         waits=[pwms[hh]] + ((p.fillw(4 + hh) + [vdv]) if pkc == 0 else []),
                                         inc="PE")
                            st["ebuf"][sl] = pv_last
                    pq.append((kc, sA, wms))
                    if len(pq) > 2:
                        flushb(False)
                while pq:
                    flushb(len(pq) == 1)
                pe_pv = pv_last
                evs = []
                for hh in range(2):
                    ev = p.emit("scalar", lambda e, hh=hh:
                                e.activation(ngb[0:65, 512 * hh:512 * (hh + 1)], pb[4 + hh][0:65, :], AF.Copy),
                                waits=[pe_pv, st["ngb"]], inc="AC")
                    p.consumed(4 + hh, ev)
                    evs.append(ev)
                if g + 1 < 8:
                    acws[g + 1] = maxpass(g + 1, qg)
                phase("BEPI")
                acw = acws[g]
                wr = wrows[par]
                for hh in range(2):
                    h = 2 * g + hh

                    def _rows1(e, hh=hh, wr=wr):
                        e.tensor_tensor(erow[0][64:65, :], ngb[64:65, 512 * hh:512 * hh + 512],
                                        wr[64:65, 512 * hh:512 * hh + 512], ALU.mult)
                        e.tensor_scalar(erow[1][64:65, :], erow[0][64:65, :], C1, C2, ALU.mult, ALU.add)
                        e.reciprocal(erow[1][64:65, :], erow[1][64:65, :])
                        return e.tensor_tensor(erow[2][64:65, :], wr[64:65, 512 * hh:512 * hh + 512],
                                               erow[1][64:65, :], ALU.mult)
                    dvr1 = p.emit("vector", _rows1, waits=[evs[hh], acw, st["erow"], st["erow_pe"]], inc="DV")
                    dvr = p.emit("vector", lambda e:
                                 e.scalar_tensor_tensor(erow[0][64:65, :], erow[0][64:65, :], 1.0e-6,
                                                        erow[1][64:65, :], ALU.add, ALU.mult), inc="DV")
                    st["erow"] = dvr
                    st["wrow64"][par] = dvr
                    mm(pb[6][0:64, :], onesr[64:65, 0:64],
                       erow[2][64:65, :], True, True,
                       waits=p.fillw(6) + [dvr1], tp=(64, 0))
                    peb = mm(pb[7][0:64, :], onesr[64:65, 0:64],
                             erow[0][64:65, :], True, True,
                             waits=p.fillw(7) + [dvr], tp=(64, 0), inc="PE")
                    st["erow_pe"] = peb

                    def _blk(e, hh=hh, h=h, g=g):
                        e.tensor_tensor(epi[0:64, 0:512], ngb[0:64, 512 * hh:512 * hh + 512],
                                        pb[6][0:64, :], ALU.mult)
                        return e.scalar_tensor_tensor(o_qg[64 * hh:64 * (hh + 1), 512 * g:512 * (g + 1)],
                                                      pb[7][0:64, :], v8c[0:64, h:h + 1],
                                                      epi[0:64, 0:512], ALU.mult, ALU.add)
                    dvb = p.emit("vector", _blk, waits=[peb, v8sy, st["epi"], st["o_qg_w"]], inc="DV")
                    st["epi"] = dvb
                    st["ebuf"][0] = dvb
                    st["ebuf"][1] = dvb
                    for bk in (6, 7):
                        p.consumed(bk, dvb)
                    st["ngb"] = dvb
                    st["v8_use"] = dvb
                    o_dvs[g] = dvb
            if qg + 1 < 4:
                acw0 = maxpass(0, qg + 1)
            emit_proj(seq, qg, False, o_dvs, xsrc, xdst, sub)
            st["o_qg_w"] = st["o_qg_r"]

    def emit_ffn(seq, xsrc, xdst, sub):
        for half in range(2):
            hb = half * 1024
            h_dv0, h_dv = emit_ln(seq, half, xsrc, ffn_mode=True)
            if paranoid >= 1:
                joinall()
            phase("FL0")
            for sc in range(8):
                par = sc % 2
                lw = Z
                for g2 in range(2):
                    lw = p.emit("sync", lambda e, sc=sc, par=par, g2=g2:
                                e.dma_start(w0b[:, 4096 * par + 2048 * g2:4096 * par + 2048 * (g2 + 1)]
                                            .rearrange("p (k c) -> p k c", k=4),
                                            w0s[4 * g2:4 * (g2 + 1), :, 512 * sc:512 * (sc + 1)]
                                            .rearrange("k p c -> p k c")),
                                waits=[st["w0b"][par], st["conv"]] if g2 == 0 else [], inc="W0%d" % par, incn=16)
                for tc in range(2):
                    for dp in range(2):
                        for dd in range(2):
                            dt_sc = 2 * dp + dd
                            ft = 4 * sc + dt_sc
                            bk = 4 * dp + 2 * dd + tc
                            pe_f = Z
                            for kk in range(DT):
                                pe_f = mm(pb[bk],
                                          w0b[:, 4096 * par + 512 * kk + 128 * dt_sc:4096 * par + 512 * kk + 128 * (dt_sc + 1)],
                                          h0T[:, 1024 * kk + 512 * tc:1024 * kk + 512 * (tc + 1)],
                                          kk == 0, kk == DT - 1,
                                          waits=(p.fillw(bk) + [lw, h_dv0 if tc == 0 else h_dv]) if kk == 0 else [],
                                          inc="PE" if kk == DT - 1 else None)
                            ac = p.emit("scalar", lambda e, bk=bk, ft=ft, tc=tc:
                                        e.activation(a1T[:, 1024 * ft + 512 * tc:1024 * ft + 512 * (tc + 1)],
                                                     pb[bk], AF.Relu, scale=sc01[:, 0:1]),
                                        waits=[pe_f], inc="AC")
                            p.consumed(bk, ac)
                st["w0b"][par] = p.v("PE")
            ac_a1 = p.v("AC")
            if paranoid >= 1:
                joinall()
            phase("FL1")
            for s in range(16):
                par = s % 2
                lw = Z
                for g2 in range(4):
                    lw = p.emit("sync", lambda e, s=s, par=par, g2=g2:
                                e.dma_start(w1b[:, 8192 * par + 2048 * g2:8192 * par + 2048 * (g2 + 1)]
                                            .rearrange("p (k c) -> p k c", k=8),
                                            w1s[s // 2, 8 * g2:8 * (g2 + 1), :, 256 * (s % 2):256 * (s % 2) + 256]
                                            .rearrange("k p c -> p k c")),
                                waits=[st["w1b"][par], st["conv"]] if g2 == 0 else [], inc="W1%d" % par, incn=16)
                if debug_stage == "ffn2" and s == 0 and half == 0:
                    p.emit("gpsimd", lambda e: e.dma_start(dbg[0, :, 0:8192], w1b[:, 0:8192]), waits=[lw], inc="XD", incn=16)
                    p.emit("gpsimd", lambda e: e.dma_start(dbg[3, 0:1, 8190:8198], srow[0:1, 0:8]), inc="XD", incn=16)
                    p.emit("gpsimd", lambda e: e.dma_start(dbg[3, 1:2, 0:128], sc01[:, 0:1]), inc="XD", incn=16)
                    p.emit("gpsimd", lambda e: e.dma_start(dbg[3, 2:3, 0:128], sc12[:, 0:1]), inc="XD", incn=16)
                for dd in range(2):
                    for tc in range(2):
                        bk = 4 * par + 2 * dd + tc
                        pe_f = Z
                        for kk in range(FT):
                            pe_f = mm(pb[bk],
                                      w1b[:, 8192 * par + 256 * kk + 128 * dd:8192 * par + 256 * kk + 128 * (dd + 1)],
                                      a1T[:, 1024 * kk + 512 * tc:1024 * kk + 512 * (tc + 1)],
                                      kk == 0, kk == FT - 1,
                                      waits=(p.fillw(bk) + [lw, ac_a1]) if kk == 0 else [],
                                      inc="PE" if kk == FT - 1 else None)
                        ac = p.emit("scalar", lambda e, bk=bk, par=par, dd=dd, tc=tc:
                                    e.activation(a2s[:, 2048 * par + 1024 * dd + 512 * tc:2048 * par + 1024 * dd + 512 * (tc + 1)],
                                                 pb[bk], AF.Relu, scale=sc12[:, 0:1]),
                                    waits=[pe_f, st["a2s"][par]], inc="AC")
                        p.consumed(bk, ac)
                st["w1b"][par] = p.v("PE")
                ac_a2 = p.v("AC")
                if debug_stage == "ffn2" and s == 1 and half == 0:
                    p.emit("gpsimd", lambda e: e.dma_start(dbg[1, :, 0:4096], a2s[:, 0:4096]), waits=[ac_a2], inc="XD", incn=16)
                if s % 2 == 1:
                    fs = s // 2
                    lw2 = p.emit("sync", lambda e, fs=fs:
                                 e.dma_start(w2b[:, :].rearrange("p (k c) -> p k c", k=4),
                                             w2s[4 * fs:4 * (fs + 1), :, :].rearrange("k p c -> p k c")),
                                 waits=[st["w2b"], st["conv"]], inc="W2", incn=16)
                    if debug_stage == "ffn2" and s == 1 and half == 0:
                        p.emit("gpsimd", lambda e: e.dma_start(dbg[2, :, 0:4096], w2b[:, 0:4096]), waits=[lw2], inc="XD", incn=16)
                    dv2 = Z
                    for tc in range(2):
                        for dt in range(DT):
                            bk = (2 * dt + tc) % 4
                            pe_f = Z
                            for k2 in range(4):
                                kpar, kd = k2 // 2, k2 % 2
                                pe_f = mm(pb[bk],
                                          w2b[:, 1024 * k2 + 128 * dt:1024 * k2 + 128 * (dt + 1)],
                                          a2s[:, 2048 * kpar + 1024 * kd + 512 * tc:2048 * kpar + 1024 * kd + 512 * (tc + 1)],
                                          k2 == 0, k2 == 3,
                                          waits=(p.fillw(bk) + [lw2, ac_a2]) if k2 == 0 else [],
                                          inc="PE" if k2 == 3 else None)
                            dv2 = p.emit("vector", lambda e, bk=bk, dt=dt, tc=tc:
                                         e.tensor_tensor(outh[:, 1024 * dt + 512 * tc:1024 * dt + 512 * (tc + 1)],
                                                         outh[:, 1024 * dt + 512 * tc:1024 * dt + 512 * (tc + 1)],
                                                         pb[bk], ALU.add),
                                         waits=[pe_f], inc="DV")
                            p.consumed(bk, dv2)
                    st["w2b"] = p.v("PE")
                    st["a2s"][0] = dv2
                    st["a2s"][1] = dv2
                    if paranoid >= 2:
                        joinall()
            if paranoid >= 1:
                joinall()
            dv_all = p.v("DV")
            sy_l = Z
            for dt in range(DT):
                sy_l = p.emit("sync", lambda e, dt=dt, hb=hb, xdst=xdst:
                              e.dma_start(xdst[seq, :, T * dt + hb:T * dt + hb + 1024],
                                          outh[:, 1024 * dt:1024 * (dt + 1)]),
                              waits=[dv_all] if dt == 0 else [], inc="XO", incn=16)
                if xdump is not None:
                    sy_l = p.emit("sync", lambda e, dt=dt, hb=hb, sub=sub:
                                  e.dma_start(xdump[sub, seq, :, T * dt + hb:T * dt + hb + 1024],
                                              outh[:, 1024 * dt:1024 * (dt + 1)]),
                                  inc="XO", incn=16)
            st["outh_sy"] = sy_l
            st["xw"] = [p.v("XO")]
            if debug_stage == "ffn2" and half == 0:
                p.emit("gpsimd", lambda e: e.dma_start(dbg[3, :, 0:8192], outh[:, 0:8192]), waits=[p.v("DV")], inc="XD", incn=16)
                return
            if debug_stage == "ffn" and half == 0:
                dq = p.v("AC")
                p.emit("gpsimd", lambda e: e.dma_start(dbg[0, :, 0:8192], h0T[:, :]), waits=[dq, p.v("DV")], inc="XD", incn=16)
                p.emit("gpsimd", lambda e: e.dma_start(dbg[1, :, 0:8192], a1T[:, 0:8192]), inc="XD", incn=16)
                p.emit("gpsimd", lambda e: e.dma_start(dbg[2, :, 0:8192], a1T[:, 24576:32768]), inc="XD", incn=16)
                p.emit("gpsimd", lambda e: e.dma_start(dbg[3, :, 0:8192], outh[:, 0:8192]), inc="XD", incn=16)
                return

    def emit_attn(seq, is_a, xsrc, xdst, sub):
        p.wait("gpsimd", [st["w1b"][0], st["w1b"][1], st["w2b"], st["w0b"][0], st["w0b"][1]])
        gsy = Z
        for half in range(2):
            hdv0, h_dv = emit_ln(seq, half, xsrc, ffn_mode=False)
            emit_qkv(seq, half, is_a, hdv0, h_dv)
            if is_a:
                gsy = emit_gate(seq, h_dv, half)
            st["hT_pe"] = p.v("PE")
        st["qkT"] = p.v("AC")
        wsrc = wout_a if is_a else wproj_b
        lwp = Z
        for kk in range(DT):
            lwp = p.emit("gpsimd", lambda e, kk=kk, wsrc=wsrc:
                         e.dma_start(wout_sb[:, 1024 * kk:1024 * (kk + 1)],
                                     wsrc[128 * kk:128 * (kk + 1), :]),
                         waits=[st["qkT"], st["wbufs"][0], st["wbufs"][1]] if kk == 0 else [], inc="WO", incn=16)
        st["wout"] = lwp
        if debug_stage == "attng":
            vdv = st["vtok"]
            for g in range(2):
                pe_g = Z
                for hh in range(2):
                    pe_g = mm(pb[hh][0:64, :], kT[64 * hh:64 * (hh + 1), T * g:T * g + 64],
                              qT[64 * hh:64 * (hh + 1), T * g:T * g + 512],
                              True, True, waits=p.fillw(hh) + [st["qkT"]],
                              tp=(64 * hh, 0), inc="PE" if hh == 1 else None)
                dvp = p.emit("vector", lambda e, g=g: e.tensor_copy(lnt[0:64, 0:512], pb[0][0:64, :]), waits=[pe_g], inc="DV")
                p.emit("gpsimd", lambda e, g=g: e.dma_start(dbg[0, 64 * g:64 * (g + 1), 0:512], lnt[0:64, 0:512]), waits=[dvp], inc="XD", incn=16)
                ac_g = []
                for hh in range(2):
                    sl = 2 + hh
                    ac = p.emit("scalar", lambda e, hh=hh, sl=sl:
                                e.activation(ebuf[0:64, 512 * sl:512 * (sl + 1)], pb[hh][0:64, :], AF.Exp),
                                waits=[pe_g, st["ebuf"][sl]], inc="AC")
                    ac_g.append(ac)
                    p.consumed(hh, ac)
                p.emit("gpsimd", lambda e, g=g: e.dma_start(dbg[1, 64 * g:64 * (g + 1), 0:1024], ebuf[0:64, 1024:2048]),
                       waits=[ac_g[1]], inc="XD", incn=16)
                for hh in range(2):
                    h = 2 * g + hh
                    pe_pvg = mm(pb[6 + hh][0:65, :], vtok[0:64, (h * 16) * 65:(h * 16) * 65 + 65],
                                ebuf[0:64, 512 * (2 + hh):512 * (2 + hh) + 512],
                                True, True, waits=p.fillw(6 + hh) + [ac_g[hh], vdv],
                                inc="PE" if hh == 1 else None)
                    st["ebuf"][2] = pe_pvg
                    st["ebuf"][3] = pe_pvg
                dvq = p.emit("vector", lambda e: e.tensor_copy(lnt[0:65, 0:512], pb[6][0:65, :]), waits=[pe_pvg], inc="DV")
                p.emit("gpsimd", lambda e, g=g: e.dma_start(dbg[2, 0:65, 512 * g:512 * (g + 1)], lnt[0:65, 0:512]), waits=[dvq], inc="XD", incn=16)
                p.consumed(6, dvq)
                p.consumed(7, dvq)
            return
        if debug_stage == "qkv":
            dq = p.v("DV")
            aq = p.v("AC")
            p.emit("gpsimd", lambda e: e.dma_start(dbg[0], qT[:, :]), waits=[dq, aq, gsy], inc="XD", incn=16)
            p.emit("gpsimd", lambda e: e.dma_start(dbg[1], kT[:, :]), inc="XD", incn=16)
            p.emit("gpsimd", lambda e: e.dma_start(dbg[2, :, 0:16384], vtok[:, 0:16384]), inc="XD", incn=16)
            p.emit("gpsimd", lambda e: e.dma_start(dbg[3, 64:65, 0:T], grow[64:65, :]), inc="XD", incn=16)
            p.emit("gpsimd", lambda e: e.dma_start(dbg[3, 64:65, T:2 * T], urow[64:65, :]), inc="XD", incn=16)
            return
        if is_a:
            emit_attn_a(seq, gsy, xsrc, xdst, sub)
        else:
            v8sy = emit_v8(seq)
            emit_attn_b(seq, v8sy, xsrc, xdst, sub)
        st["xw"] = [p.v("XS0"), p.v("XS1")]

    for seq in range(seqs):
        for sub in range(n_sub):
            kind = chain[sub]
            xsrc = xin if sub == 0 else xbuf
            xdst = y if sub == n_sub - 1 else xbuf
            if kind == "ffn":
                emit_ffn(seq, xsrc, xdst, sub)
            else:
                emit_attn(seq, kind == "attn_a", xsrc, xdst, sub)

    if dump_x:
        marks = [p.v(x) for x in ("PE", "DV", "AC") + DMAS]
        p.wait("gpsimd", marks)
        p.emit("gpsimd", lambda e: e.dma_start(enddump[0, :, 0:128], smallp[:, :].bitcast(F32)[:, 0:128]), inc="XD", incn=16)
        p.emit("gpsimd", lambda e: e.dma_start(enddump[1, 0:1, 0:128], sc01[:, 0:1]), inc="XD", incn=16)
        p.emit("gpsimd", lambda e: e.dma_start(enddump[2, 0:1, 0:128], sc12[:, 0:1]), inc="XD", incn=16)
        p.emit("gpsimd", lambda e: e.dma_start(enddump[3, :, 0:512], bv_b[:, :].bitcast(F32)[:, 0:512]), inc="XD", incn=16)
        p.emit("gpsimd", lambda e: e.dma_start(enddump[4, :, 0:128], wg_bf[:, :].bitcast(F32)[:, 0:128]), inc="XD", incn=16)
        p.emit("gpsimd", lambda e: e.dma_start(enddump[5, :, 0:1024], band[:, :].bitcast(F32)[:, 0:1024]), inc="XD", incn=16)
        p.emit("gpsimd", lambda e: e.dma_start(enddump[6, :, 0:512], bandf[:, :].bitcast(F32)[:, 0:512]), inc="XD", incn=16)
        p.emit("gpsimd", lambda e: e.dma_start(enddump[7, :, 0:128], onesr[:, :].bitcast(F32)[:, 0:32].bitcast(F32)) if False else e.dma_start(enddump[7, :, 0:128], onesr[:, 0:128]), inc="XD", incn=16)
    final = [p.v(x) for x in DMAS] + ([p.v("XD")] if dump_x else [])
    p.wait("sync", final)
    p.wait("gpsimd", final)

    with nc.Block() as block:
        @block.tensor
        def _(t):
            for f in p.ops["tensor"]:
                f(t)

        @block.vector
        def _(v):
            for f in p.ops["vector"]:
                f(v)

        @block.scalar
        def _(s):
            for f in p.ops["scalar"]:
                f(s)

        @block.gpsimd
        def _(g):
            for f in p.ops["gpsimd"]:
                f(g)

        @block.sync
        def _(s):
            for f in p.ops["sync"]:
                f(s)
    return nc



def make_inputs(inputs, seqs_per_core=4):
    x = np.ascontiguousarray(np.asarray(inputs["L0_attn_a_hidden_states"]).transpose(0, 2, 1))
    B = x.shape[0]
    k0 = np.asarray(inputs["L0_ffn_k0"])
    com = {
        "wqkv_a": np.ascontiguousarray(np.asarray(inputs["L0_attn_a_Wqkv"], np.float32)),
        "bqkv_a": np.ascontiguousarray(np.asarray(inputs["L0_attn_a_bqkv"], np.float32).reshape(3 * D, 1)),
        "wout_a": np.ascontiguousarray(np.asarray(inputs["L0_attn_a_Wout"], np.float32)),
        "bout_a": np.ascontiguousarray(np.asarray(inputs["L0_attn_a_bout"], np.float32).reshape(D, 1)),
        "wg_a": np.ascontiguousarray(np.asarray(inputs["L0_attn_a_Wg"], np.float32)),
        "bg_a": np.ascontiguousarray(np.asarray(inputs["L0_attn_a_bg"], np.float32).reshape(1, 1)),
        "wqkv_b": np.ascontiguousarray(np.asarray(inputs["L0_attn_b_w_qkv"], np.float32).T),
        "wproj_b": np.ascontiguousarray(np.asarray(inputs["L0_attn_b_w_proj"], np.float32).T),
        "k0a": np.ascontiguousarray(k0[:D].astype(np.uint8)),
        "k0b": np.ascontiguousarray(k0[D:].astype(np.uint8)),
        "k1": np.ascontiguousarray(np.asarray(inputs["L0_ffn_k1"]).astype(np.uint8)),
        "k2": np.ascontiguousarray(np.asarray(inputs["L0_ffn_k2"]).astype(np.uint8)),
        "wg8": np.ascontiguousarray(np.asarray(inputs["L0_attn_a_Wg"], np.float32).reshape(8, 128).T),
        "s0": np.asarray(inputs["L0_ffn_s0"], np.float32).reshape(1, 1),
        "s1": np.asarray(inputs["L0_ffn_s1"], np.float32).reshape(1, 1),
        "s2": np.asarray(inputs["L0_ffn_s2"], np.float32).reshape(1, 1),
    }
    sp = np.zeros((128, 32), np.float32)
    sp[:, 0:8] = np.asarray(inputs["L0_attn_a_bqkv"], np.float32)[0:1024].reshape(8, 128).T
    sp[:, 8:16] = np.asarray(inputs["L0_attn_a_bqkv"], np.float32)[1024:2048].reshape(8, 128).T
    sp[:, 16:24] = np.asarray(inputs["L0_attn_a_bout"], np.float32).reshape(8, 128).T
    sp[0, 24] = np.float32(np.asarray(inputs["L0_ffn_s0"]).reshape(()))
    sp[0, 25] = np.float32(np.asarray(inputs["L0_ffn_s1"]).reshape(()))
    sp[0, 26] = np.float32(np.asarray(inputs["L0_ffn_s2"]).reshape(()))
    sp[0, 27] = np.float32(np.asarray(inputs["L0_attn_a_bg"]).reshape(()))
    com["smallpack"] = np.ascontiguousarray(sp)
    com.update(host_constants())
    ncores = B // seqs_per_core
    in_maps = []
    for c in range(ncores):
        im = dict(com)
        xs = x[c * seqs_per_core:(c + 1) * seqs_per_core]
        im["x"] = np.ascontiguousarray(
            xs.reshape(seqs_per_core, DT, 128, T).transpose(0, 2, 1, 3).reshape(seqs_per_core, 128, DT * T))
        in_maps.append(im)
    return in_maps, ncores


def unpack_y(res_list, seqs_per_core=4):
    outs = []
    for r in res_list:
        yv = np.asarray(r["y"])
        yv = yv.reshape(seqs_per_core, 128, DT, T).transpose(0, 2, 1, 3).reshape(seqs_per_core, D, T)
        outs.append(yv.transpose(0, 2, 1))
    return np.concatenate(outs, axis=0).astype(np.float32)


def kernel(**inputs):
    in_maps, ncores = make_inputs(inputs)
    nc = bass.Bass("TRN2", target_bir_lowering=False, detect_race_conditions=False)
    build_core(nc, n_sub=4, seqs=4)
    res = run_bass_kernel_spmd(nc, in_maps, core_ids=list(range(ncores)))
    return unpack_y(res.results)
